# version 04
# baseline (speedup 1.0000x reference)
import contextlib
import numpy as np
import ml_dtypes
import concourse.bass as bass
import concourse.mybir as mybir
from concourse.bass_utils import run_bass_kernel_spmd

F32 = mybir.dt.float32
BF16 = mybir.dt.bfloat16
I32 = mybir.dt.int32
AF = mybir.ActivationFunctionType
ALU = mybir.AluOpType
AX = mybir.AxisListType

NCORES = 8
B, T, D = 2, 8192, 1024
NTOK = B * T
TPC = NTOK // NCORES
NTILE = NTOK // 128
H, HD = 16, 64
DL = 512
EXP = 8
CAP = 4096
LN_EPS = 1e-5
DI = 2048
RMS_EPS = 1.1920929e-07
OOB = 999999.0

bfdt = ml_dtypes.bfloat16


class Prog:
    def __init__(self):
        self.nc = bass.Bass("TRN2", target_bir_lowering=False, dynamic_dma_scratch_size=49152)
        self.stack = contextlib.ExitStack()
        self.sems = {}
        self.semv = {}

    def sem(self, name):
        if name not in self.sems:
            self.sems[name] = self.stack.enter_context(self.nc.semaphore(name))
            self.semv[name] = 0
        return self.sems[name]

    def inc(self, instr, name, n=1):
        instr.then_inc(self.sem(name), n)
        self.semv[name] += n
        return self.semv[name]

    def val(self, name):
        self.sem(name)
        return self.semv[name]

    def sb(self, name, shape, dtype=F32):
        return self.stack.enter_context(self.nc.sbuf_tensor(name, shape, dtype))

    def ps(self, name, shape, dtype=F32):
        h = self.stack.enter_context(self.nc.psum_tensor(name, shape, dtype))
        if not hasattr(self, "psum_ids"):
            self.psum_ids = set()
        self.psum_ids.add(id(h))
        return h

    def close(self):
        self.stack.close()





ENGS = ["sync", "gpsimd", "vector", "scalar", "tensor"]


class Op:
    __slots__ = ["eng", "fn", "reads", "writes", "dma", "idx", "waits", "tag",
                 "_semk", "_sidx"]

    def __init__(self, eng, fn, reads, writes, dma, tag=""):
        self.eng = eng; self.fn = fn; self.dma = dma; self.tag = tag
        self.reads = reads; self.writes = writes
        self.idx = None; self.waits = []


def _keys(lst):
    out = []
    for it in (lst or []):
        if isinstance(it, tuple):
            out.append((id(it[0]), it[1]))
        else:
            out.append((id(it), ""))
    return out


class Sched:
    def __init__(self, p):
        self.p = p
        self.ops = []
        self.eng_count = {e: 0 for e in ENGS}

    def add(self, eng, fn, r=None, w=None, dma=False, tag=""):
        self.ops.append(Op(eng, fn, _keys(r), _keys(w), dma, tag))

    def run(self):
        last_w = {}
        last_r = {}
        id_tags = {}
        for op in self.ops:
            deps = {}

            def edge(dep, op=op, deps=deps):
                if dep is None or dep is op:
                    return
                if dep._semk == op._semk == "tensor":
                    return
                if deps.get(dep._semk, 0) < dep._sidx + 1:
                    deps[dep._semk] = dep._sidx + 1

            def wkeys(key):
                i, t = key
                tags = id_tags.get(i, set())
                return [(i, tt) for tt in (tags | {"", t})]

            psids = getattr(self.p, "psum_ids", set())
            for key in op.reads:
                for k2 in wkeys(key):
                    edge(last_w.get(k2))
                if key[0] in psids:
                    for k2 in wkeys(key):
                        for rd in last_r.get(k2, []):
                            if rd._semk != op._semk:
                                edge(rd)
            for key in op.writes:
                for k2 in wkeys(key):
                    edge(last_w.get(k2))
                    for rd in last_r.get(k2, []):
                        edge(rd)
            op.waits = deps
            for key in op.writes:
                id_tags.setdefault(key[0], set()).add(key[1])
                last_w[key] = op
                last_r[key] = []
            for key in op.reads:
                id_tags.setdefault(key[0], set()).add(key[1])
                last_r.setdefault(key, []).append(op)

    def prepare(self):
        for op in self.ops:
            semk = op.eng + ("_dma" if op.dma else "")
            op._semk = semk
        ccount = {}
        for op in self.ops:
            op._sidx = ccount.get(op._semk, 0)
            ccount[op._semk] = op._sidx + 1
        self.run()

    def emit(self, blk):
        p = self.p
        per_eng = {e: [op for op in self.ops if op.eng == e] for e in ENGS}
        done = {e: {} for e in ENGS}

        def mk(engname):
            oplist = per_eng[engname]

            def body(eng):
                seen = done[engname]
                for op in oplist:
                    if op.dma and op._sidx > 0:
                        if seen.get(op._semk, 0) < op._sidx:
                            eng.wait_ge(p.sem("S_" + op._semk), op._sidx * 16)
                            seen[op._semk] = op._sidx
                    for k, v in sorted(op.waits.items()):
                        if seen.get(k, 0) >= v:
                            continue
                        mult = 16 if k.endswith("_dma") else 1
                        eng.wait_ge(p.sem("S_" + k), v * mult)
                        seen[k] = v
                    ii = op.fn(eng)
                    p.inc(ii, "S_" + op._semk, 16 if op.dma else 1)
            return body

        for engname in ENGS:
            if per_eng[engname]:
                getattr(blk, engname)(mk(engname))



def bcast8(t, nparts=128, segs=16):
    ap = t[:]
    return bass.AP(ap.tensor, ap.offset, [ap.ap[0], [1, segs], [0, 8]])


def seg3(ap_2d, e=8):
    return ap_2d.rearrange("p (s e) -> p s e", e=e)


def emit_prep(p, x, wr, xn_all, routing_out, inv_out, inv_part, inv_red,
              selmask, tokid_pc):
    nc = p.nc
    NB = 4
    xbuf = [p.sb(f"pa_x{i}", [128, D]) for i in range(NB)]
    xnbuf = [p.sb(f"pa_xn{i}", [128, D], BF16) for i in range(NB)]
    sq_scr = p.sb("pa_sq", [128, D])
    fence_s = p.sb("pa_fence", [128, 8])
    sums_all = p.sb("pa_sums", [128, NTILE])
    ssq_all = p.sb("pa_ssq", [128, NTILE])
    mu_all = p.sb("pa_mu", [128, NTILE])
    var_all = p.sb("pa_var", [128, NTILE])
    rstd_all = p.sb("pa_rstd", [128, NTILE])
    nmr_all = p.sb("pa_nmr", [128, NTILE])

    logT_d = nc.dram_tensor("pa_logT_d", [8, NTOK], F32, kind="Internal").ap()
    oh1_d = nc.dram_tensor("pa_oh1_d", [128, NTOK // 16], F32, kind="Internal").ap()
    rout_d = nc.dram_tensor("pa_rout_d", [128, 512], F32, kind="Internal").ap()
    rout_all_d = nc.dram_tensor("pa_rout_all_d", [128 * NCORES, 512], F32, kind="Internal").ap()
    oh2_d = nc.dram_tensor("pa_oh2_d", [128, NTOK // 16], F32, kind="Internal").ap()
    indT_d = nc.dram_tensor("pa_indT_d", [8, NTOK], F32, kind="Internal").ap()
    exclT_d = nc.dram_tensor("pa_exclT_d", [8, NTOK], F32, kind="Internal").ap()
    logstrip = p.sb("pa_logstrip", [8, 2048])
    sc_in = p.sb("pa_scin", [8, 2048])
    sc_out = p.sb("pa_scout", [8, 2048])
    sc_carry = p.sb("pa_sccarry", [8, 2])
    zer2k = p.sb("pa_zer2k", [8, 2048])
    wr_f = p.sb("pa_wrf", [128, 64])
    wr_b = p.sb("pa_wrb", [128, 64], BF16)
    xnT = [p.sb(f"pa_xnT{i}", [128, 2048], BF16) for i in range(2)]
    identf = p.sb("pa_identf", [128, 128])
    ones128 = p.sb("pa_ones128", [128, 128])
    lgin = [p.sb(f"pa_lgin{i}", [128, 128]) for i in range(2)]
    lgb = p.sb("pa_lgb", [128, 128])
    emb = p.sb("pa_emb", [128, 128])
    prb = p.sb("pa_prb", [128, 128])
    oh1b = p.sb("pa_oh1b", [128, 128])
    oh2b = p.sb("pa_oh2b", [128, 128])
    indb = p.sb("pa_indb", [128, 128])
    exin = [p.sb(f"pa_exin{i}", [128, 128]) for i in range(2)]
    exb = p.sb("pa_exb", [128, 128])
    exbase = p.sb("pa_exbase", [128, 128])
    scrb = p.sb("pa_scrb", [128, 128])
    ecb = p.sb("pa_ecb", [128, 128])
    ecm = p.sb("pa_ecm", [128, 128])
    bas = p.sb("pa_bas", [128, 128])
    st16 = p.sb("pa_st16", [128, 16 * 16])
    rout = p.sb("pa_rout", [128, 512])
    selmask_sb = p.sb("pa_selmsb", [128, 2048])
    tokpc_sb = p.sb("pa_tokpc", [128, 16])
    zrow = p.sb("pa_zrow", [128, 256])

    ps_l = [p.ps(f"pa_psl{i}", [8, 512]) for i in range(4)]
    ps_t = [p.ps(f"pa_pst{i}", [128, 128]) for i in range(2)]
    ps_i = [p.ps(f"pa_psi{i}", [128, 128]) for i in range(2)]

    def s16(i):
        return st16[:, 16 * i:16 * (i + 1)]

    def vch(v, ii):
        n = p.inc(ii, "pa_vch")
        v.wait_ge(p.sem("pa_vch"), n)

    NBATCH = NTILE // 16
    slot_ib = [[p.sb(f"pa_sib{i}_{k}", [128, 1], I32) for k in range(2)] for i in range(2)]
    slot_fb = [[p.sb(f"pa_sfb{i}_{k}", [128, 1]) for k in range(2)] for i in range(2)]
    tokv = [p.sb(f"pa_tokv{i}", [128, 1]) for i in range(2)]
    mskscr = p.sb("pa_mskscr", [128, 128])

    with nc.Block() as blk:
        @blk.sync
        def _(sync):
            for j in range(NTILE):
                if j >= NB:
                    sync.wait_ge(p.sem("pa_rs"), j - NB + 1)
                    sync.wait_ge(p.sem("pa_sqd"), j - NB + 1)
                ii = sync.dma_start(out=xbuf[j % NB][:],
                                    in_=x[128 * j:128 * (j + 1), :])
                p.inc(ii, "pa_ld", 16)
            for j in range(NTILE):
                if j < NB:
                    sync.wait_ge(p.sem("pa_rs"), NTILE - NB + 1 + j)
                    sync.wait_ge(p.sem("pa_sqd"), NTILE - NB + 1 + j)
                else:
                    sync.wait_ge(p.sem("pa_xnw"), j - NB + 1)
                ii = sync.dma_start(out=xbuf[j % NB][:],
                                    in_=x[128 * j:128 * (j + 1), :])
                p.inc(ii, "pa_ld2", 16)

            sync.wait_ge(p.sem("pa_init"), 1)
            ii = sync.dma_start(out=wr_f[:].rearrange("p (c e) -> p c e", c=8),
                                in_=wr.rearrange("(c p) e -> p c e", p=128))
            p.inc(ii, "pa_wr", 16)
            sync.wait_ge(p.sem("pa_xn_st"), 16 * NTILE)
            for tb in range(8):
                for dc in range(8):
                    k = tb * 8 + dc
                    if k >= 2:
                        sync.wait_ge(p.sem("pa_lmm"), (k - 1))
                    ii = sync.dma_start(
                        out=xnT[k % 2][:],
                        in_=xn_all[2048 * tb:2048 * (tb + 1), 128 * dc:128 * (dc + 1)],
                        transpose=True)
                    p.inc(ii, "pa_xt", 16)

            for gb in range(NBATCH):
                sync.wait_ge(p.sem("pa_lst"), 16 * (gb + 1))
                if gb >= 2:
                    sync.wait_ge(p.sem("pa_l1tr"), gb - 1)
                ii = sync.dma_start(
                    out=lgin[gb % 2][:],
                    in_=logT_d[:, 2048 * gb:2048 * (gb + 1)]
                        .rearrange("e (jj pp) -> jj e pp", jj=16))
                p.inc(ii, "pa_lgld", 16)
            for c in range(8):
                sync.wait_ge(p.sem("pa_indst"), 16 * NBATCH)
                if c >= 1:
                    sync.wait_ge(p.sem("pa_scu"), c)
                ii = sync.dma_start(out=sc_in[:],
                                    in_=indT_d[:, 2048 * c:2048 * (c + 1)])
                p.inc(ii, "pa_scld", 16)
            for gb in range(NBATCH):
                sync.wait_ge(p.sem("pa_scst"), 16 * 8)
                if gb >= 2:
                    sync.wait_ge(p.sem("pa_l2tr"), gb - 1)
                ii = sync.dma_start(
                    out=exin[gb % 2][:],
                    in_=exclT_d[:, 2048 * gb:2048 * (gb + 1)]
                        .rearrange("e (jj pp) -> jj e pp", jj=16))
                p.inc(ii, "pa_exld", 16)

            ii = sync.dma_start(out=selmask_sb[:], in_=selmask)
            p.inc(ii, "pa_selm", 16)
            ii = sync.dma_start(out=tokpc_sb[:], in_=tokid_pc)
            p.inc(ii, "pa_selm", 16)


        @blk.gpsimd
        def _(g):
            for j in range(NTILE):
                g.wait_ge(p.sem("pa_xnw"), j + 1)
                ii = g.dma_start(out=xn_all[128 * j:128 * (j + 1), :],
                                 in_=xnbuf[j % NB][:])
                p.inc(ii, "pa_xn_st", 16)

            for tb in range(8):
                g.wait_ge(p.sem("pa_lev"), tb * 4 + 4)
                ii = g.dma_start(out=logT_d[:, 2048 * tb:2048 * (tb + 1)],
                                 in_=logstrip[:])
                p.inc(ii, "pa_lst", 16)

            g.wait_ge(p.sem("pa_init"), 1)
            g.affine_select(identf[:], ones128[:], [[-1, 128]], ALU.is_equal, 0.0,
                            base=0, channel_multiplier=1)
            ii = g.iota(ecb[:].rearrange("p (s e) -> p s e", e=8), [[0, 16], [1, 8]],
                        base=0, channel_multiplier=0,
                        allow_small_or_imprecise_dtypes=True)
            p.inc(ii, "pa_gini", 1)
            for gb in range(NBATCH):
                g.wait_ge(p.sem("pa_indev"), gb + 1)
                ii = g.dma_start(
                    out=indT_d[:, 2048 * gb:2048 * (gb + 1)]
                        .rearrange("e (jj pp) -> jj e pp", jj=16),
                    in_=indb[:])
                p.inc(ii, "pa_indst", 16)
            for c in range(8):
                g.wait_ge(p.sem("pa_scu"), c + 1)
                ii = g.dma_start(out=exclT_d[:, 2048 * c:2048 * (c + 1)],
                                 in_=sc_out[:])
                p.inc(ii, "pa_scst", 16)

            g.wait_ge(p.sem("pa_r2"), 2 * (NTILE // 16))
            ii = g.dma_start(out=rout_d, in_=rout[:])
            p.inc(ii, "pa_rd", 16)
            g.wait_ge(p.sem("pa_rd"), 16)
            ii = g.collective_compute("AllGather", ALU.bypass, [list(range(NCORES))],
                                      [rout_d.opt()], [rout_all_d.opt()])
            p.inc(ii, "pa_rcc", 1)
            g.wait_ge(p.sem("pa_rcc"), 1)
            ii = g.dma_start(out=rout[:], in_=rout_all_d[0:128, :])
            p.inc(ii, "pa_rbc", 16)
            g.wait_ge(p.sem("pa_init"), 1)
            ii = g.dma_start(out=inv_part.rearrange("(a b) o -> a (b o)", a=128),
                             in_=zrow[:])
            p.inc(ii, "pa_invz", 16)
            g.wait_ge(p.sem("pa_invz"), 16)
            for jj in range(16):
                g.wait_ge(p.sem("pa_slot"), jj + 1)
                for k in range(2):
                    ii = g.indirect_dma_start(
                        inv_part, bass.IndirectOffsetOnAxis(slot_ib[jj % 2][k][:], 0),
                        tokv[jj % 2][:], None,
                        bounds_check=CAP * EXP - 1, oob_is_err=False)
                    p.inc(ii, "pa_inv", 16)
            g.wait_ge(p.sem("pa_inv"), 32 * 16)
            ii = g.collective_compute("AllReduce", ALU.add, [list(range(NCORES))],
                                      [inv_part.opt()], [inv_red.opt()])
            p.inc(ii, "pa_cc", 1)
            g.wait_ge(p.sem("pa_cc"), 1)
            ii = g.dma_start(out=inv_out, in_=inv_red)
            p.inc(ii, "pa_invst", 16)


        @blk.vector
        def _(v):
            v.memset(zrow[:], 0.0)
            v.memset(zer2k[:], 0.0)
            ii = v.memset(ones128[:], 1.0)
            p.inc(ii, "pa_init", 1)
            for j in range(NTILE):
                v.wait_ge(p.sem("pa_ld"), 16 * (j + 1))
                ii = v.reduce_sum(sums_all[:, j:j + 1], xbuf[j % NB][:], axis=AX.X)
                p.inc(ii, "pa_rs", 1)
            v.wait_ge(p.sem("pa_sqd"), NTILE)
            ii = v.tensor_scalar(mu_all[:], sums_all[:], 1.0 / D, None, ALU.mult)
            vch(v, ii)
            ii = v.tensor_tensor(var_all[:], mu_all[:], mu_all[:], ALU.mult)
            vch(v, ii)
            ii = v.tensor_scalar(ssq_all[:], ssq_all[:], 1.0 / D, None, ALU.mult)
            vch(v, ii)
            ii = v.tensor_sub(var_all[:], ssq_all[:], var_all[:])
            vch(v, ii)
            ii = v.tensor_scalar(var_all[:], var_all[:], LN_EPS, None, ALU.add)
            ii = p.inc(ii, "pa_varr", 1)
            v.wait_ge(p.sem("pa_stdd"), 1)
            ii = v.reciprocal(rstd_all[:], var_all[:])
            vch(v, ii)
            ii = v.tensor_tensor(nmr_all[:], mu_all[:], rstd_all[:], ALU.mult)
            vch(v, ii)
            ii = v.tensor_scalar(nmr_all[:], nmr_all[:], -1.0, None, ALU.mult)
            p.inc(ii, "pa_fin", 1)

            v.wait_ge(p.sem("pa_wr"), 16)
            ii = v.tensor_copy(wr_b[:], wr_f[:])
            p.inc(ii, "pa_wrb", 1)
            for tb in range(8):
                v.wait_ge(p.sem("pa_lmm"), tb * 8 + 8)
                if tb >= 1:
                    v.wait_ge(p.sem("pa_lst"), 16 * tb)
                for q in range(4):
                    ii = v.tensor_copy(logstrip[:, 512 * q:512 * (q + 1)], ps_l[q][:])
                    p.inc(ii, "pa_lev", 1)

            v.wait_ge(p.sem("pa_gini"), 1)
            ii = v.tensor_scalar(ecm[:], ecb[:], -99.0, None, ALU.add)
            vch(v, ii)
            ii = v.tensor_scalar(bas[:], ecb[:], float(CAP), None, ALU.mult)
            vch(v, ii)
            for gb in range(NBATCH):
                v.wait_ge(p.sem("pa_l1tr"), gb + 1)
                ii = v.tensor_copy(lgb[:], ps_t[gb % 2][:])
                p.inc(ii, "pa_lgbev", 1)
                v.wait_ge(p.sem("pa_lgbev"), p.semv["pa_lgbev"])
                ii = v.reduce_max(s16(0), seg3(lgb[:]), axis=AX.X)
                vch(v, ii)
                ii = v.tensor_tensor(seg3(emb[:]), seg3(lgb[:]), bcast8(st16),
                                     ALU.subtract)
                ii = p.inc(ii, "pa_sub1", 1)
                v.wait_ge(p.sem("pa_exp"), gb + 1)
                ii = v.reduce_sum(s16(1), seg3(emb[:]), axis=AX.X)
                vch(v, ii)
                ii = v.reciprocal(s16(2), s16(1))
                vch(v, ii)
                sb2 = bass.AP(st16[:].tensor, st16[:].offset + 32,
                              [st16[:].ap[0], [1, 16], [0, 8]])
                ii = v.tensor_tensor(seg3(prb[:]), seg3(emb[:]), sb2, ALU.mult)
                vch(v, ii)
                ii = v.reduce_max(s16(3), seg3(prb[:]), axis=AX.X)
                vch(v, ii)
                sb3 = bass.AP(st16[:].tensor, st16[:].offset + 48,
                              [st16[:].ap[0], [1, 16], [0, 8]])
                v.wait_ge(p.sem("pa_ohst"), 32 * gb)
                ii = v.tensor_tensor(seg3(oh1b[:]), seg3(prb[:]), sb3, ALU.is_ge)
                vch(v, ii)
                ii = v.tensor_mul(scrb[:], oh1b[:], ecm[:])
                vch(v, ii)
                ii = v.tensor_scalar(scrb[:], scrb[:], 99.0, None, ALU.add)
                vch(v, ii)
                ii = v.tensor_reduce(s16(4), seg3(scrb[:]), axis=AX.X, op=ALU.min)
                vch(v, ii)
                sb4 = bass.AP(st16[:].tensor, st16[:].offset + 64,
                              [st16[:].ap[0], [1, 16], [0, 8]])
                ii = v.tensor_tensor(seg3(oh1b[:]), seg3(ecb[:]), sb4, ALU.is_equal)
                vch(v, ii)
                ii = v.scalar_tensor_tensor(scrb[:], oh1b[:], -1e9, prb[:],
                                            ALU.mult, ALU.add)
                vch(v, ii)
                ii = v.reduce_max(s16(5), seg3(scrb[:]), axis=AX.X)
                vch(v, ii)
                sb5 = bass.AP(st16[:].tensor, st16[:].offset + 80,
                              [st16[:].ap[0], [1, 16], [0, 8]])
                ii = v.tensor_tensor(seg3(oh2b[:]), seg3(scrb[:]), sb5, ALU.is_ge)
                vch(v, ii)
                ii = v.tensor_mul(scrb[:], oh2b[:], ecm[:])
                vch(v, ii)
                ii = v.tensor_scalar(scrb[:], scrb[:], 99.0, None, ALU.add)
                vch(v, ii)
                ii = v.tensor_reduce(s16(6), seg3(scrb[:]), axis=AX.X, op=ALU.min)
                vch(v, ii)
                sb6 = bass.AP(st16[:].tensor, st16[:].offset + 96,
                              [st16[:].ap[0], [1, 16], [0, 8]])
                ii = v.tensor_tensor(seg3(oh2b[:]), seg3(ecb[:]), sb6, ALU.is_equal)
                vch(v, ii)
                v.wait_ge(p.sem("pa_indst"), 16 * gb)
                ii = v.tensor_add(indb[:], oh1b[:], oh2b[:])
                p.inc(ii, "pa_indrdy", 1)
                v.wait_ge(p.sem("pa_indrdy"), p.semv["pa_indrdy"])
                ii = v.tensor_copy(rout[:, 16 * gb:16 * (gb + 1)], s16(3))
                vch(v, ii)
                ii = v.tensor_copy(rout[:, 128 + 16 * gb:128 + 16 * (gb + 1)], s16(5))
                vch(v, ii)
                v.wait_ge(p.sem("pa_i1tr"), gb + 1)
                ii = v.tensor_copy(indb[:], ps_i[gb % 2][:])
                p.inc(ii, "pa_indev", 1)
                v.wait_ge(p.sem("pa_indev"), p.semv["pa_indev"])
            v.memset(sc_carry[:], 0.0)
            for c in range(8):
                v.wait_ge(p.sem("pa_scld"), 16 * (c + 1))
                if c >= 1:
                    v.wait_ge(p.sem("pa_scst"), 16 * c)
                ii = v.tensor_tensor_scan(sc_out[:], sc_in[:], zer2k[:],
                                          sc_carry[:, 0:1], ALU.add, ALU.add)
                vch(v, ii)
                ii = v.tensor_copy(sc_carry[:, 0:1], sc_out[:, 2047:2048])
                vch(v, ii)
                ii = v.tensor_sub(sc_out[:], sc_out[:], sc_in[:])
                p.inc(ii, "pa_scu", 1)
                v.wait_ge(p.sem("pa_scu"), p.semv["pa_scu"])
            for gb in range(NBATCH):
                v.wait_ge(p.sem("pa_l2tr"), gb + 1)
                ii = v.tensor_copy(exb[:], ps_t[gb % 2][:])
                p.inc(ii, "pa_exbev", 1)
                v.wait_ge(p.sem("pa_exbev"), p.semv["pa_exbev"])
                v.wait_ge(p.sem("pa_ohld"), 32 * (gb + 1))
                ii = v.tensor_add(scrb[:], exb[:], bas[:])
                vch(v, ii)
                for k in range(2):
                    ohb = oh1b if k == 0 else oh2b
                    ii = v.tensor_mul(emb[:], ohb[:], exb[:])
                    vch(v, ii)
                    ii = v.reduce_sum(s16(7 + k), seg3(emb[:]), axis=AX.X)
                    vch(v, ii)
                    ii = v.tensor_mul(emb[:], ohb[:], scrb[:])
                    vch(v, ii)
                    ii = v.reduce_sum(s16(9 + k), seg3(emb[:]), axis=AX.X)
                    vch(v, ii)
                    ii = v.tensor_scalar(s16(11 + k), s16(7 + k), float(CAP), None,
                                         ALU.is_lt)
                    vch(v, ii)
                    ii = v.tensor_mul(rout[:, 128 * k + 16 * gb:128 * k + 16 * (gb + 1)],
                                      rout[:, 128 * k + 16 * gb:128 * k + 16 * (gb + 1)],
                                      s16(11 + k))
                    vch(v, ii)
                    ii = v.tensor_scalar(s16(13), s16(9 + k), -OOB, None, ALU.add)
                    vch(v, ii)
                    ii = v.tensor_mul(s16(13), s16(13), s16(11 + k))
                    vch(v, ii)
                    ii = v.tensor_scalar(
                        rout[:, 256 + 128 * k + 16 * gb:256 + 128 * k + 16 * (gb + 1)],
                        s16(13), OOB, None, ALU.add)
                    p.inc(ii, "pa_r2", 1)
                    v.wait_ge(p.sem("pa_r2"), p.semv["pa_r2"])

            v.wait_ge(p.sem("pa_rbc"), 16)
            v.wait_ge(p.sem("pa_selm"), 32)
            for jj in range(16):
                if jj >= 2:
                    v.wait_ge(p.sem("pa_inv"), 32 * (jj - 1))
                for k in range(2):
                    ii = v.tensor_mul(mskscr[:], rout[:, 256 + 128 * k:384 + 128 * k],
                                      selmask_sb[:, 128 * jj:128 * (jj + 1)])
                    n = p.inc(ii, "pa_vch"); v.wait_ge(p.sem("pa_vch"), n)
                    ii = v.reduce_sum(slot_fb[jj % 2][k][:], mskscr[:], axis=AX.X)
                    n = p.inc(ii, "pa_vch"); v.wait_ge(p.sem("pa_vch"), n)
                    v.tensor_copy(slot_ib[jj % 2][k][:], slot_fb[jj % 2][k][:])
                ii = v.tensor_copy(tokv[jj % 2][:], tokpc_sb[:, jj:jj + 1])
                p.inc(ii, "pa_slot", 1)


        @blk.scalar
        def _(s):
            for j in range(NTILE):
                s.wait_ge(p.sem("pa_ld"), 16 * (j + 1))
                ii = s.activation(sq_scr[:], xbuf[j % NB][:], AF.Square,
                                  accum_out=ssq_all[:, j:j + 1])
                p.inc(ii, "pa_sqd", 1)
            s.wait_ge(p.sem("pa_varr"), 1)
            ii = s.activation(var_all[:], var_all[:], AF.Sqrt)
            p.inc(ii, "pa_stdd", 1)
            s.wait_ge(p.sem("pa_fin"), 1)
            for j in range(NTILE):
                s.wait_ge(p.sem("pa_ld2"), 16 * (j + 1))
                if j >= NB:
                    s.wait_ge(p.sem("pa_xn_st"), 16 * (j - NB + 1))
                s.activation(xnbuf[j % NB][:], xbuf[j % NB][:], AF.Identity,
                             bias=nmr_all[:, j:j + 1], scale=rstd_all[:, j:j + 1])
                ii = s.activation(fence_s[:], xnbuf[j % NB][:, D - 8:D], AF.Copy)
                p.inc(ii, "pa_xnw", 1)

            for gb in range(NBATCH):
                s.wait_ge(p.sem("pa_sub1"), gb + 1)
                ii = s.activation(emb[:], emb[:], AF.Exp)
                p.inc(ii, "pa_exp", 1)
                s.wait_ge(p.sem("pa_indrdy"), gb + 1)
                ii = s.dma_start(out=oh1_d[:, 128 * gb:128 * (gb + 1)], in_=oh1b[:])
                p.inc(ii, "pa_ohst", 16)
                ii = s.dma_start(out=oh2_d[:, 128 * gb:128 * (gb + 1)], in_=oh2b[:])
                p.inc(ii, "pa_ohst", 16)
            for gb in range(NBATCH):
                s.wait_ge(p.sem("pa_r2"), 2 * gb)
                ii = s.dma_start(out=oh1b[:], in_=oh1_d[:, 128 * gb:128 * (gb + 1)])
                p.inc(ii, "pa_ohld", 16)
                ii = s.dma_start(out=oh2b[:], in_=oh2_d[:, 128 * gb:128 * (gb + 1)])
                p.inc(ii, "pa_ohld", 16)

            s.wait_ge(p.sem("pa_rbc"), 16)
            ii = s.dma_start(out=routing_out, in_=rout[:])
            p.inc(ii, "pa_rst", 16)
            s.wait_ge(p.sem("pa_rst"), 16)
            s.wait_ge(p.sem("pa_invst"), 16)
            s.wait_ge(p.sem("pa_xn_st"), 16 * NTILE)


        @blk.tensor
        def _(t):
            t.wait_ge(p.sem("pa_wrb"), 1)
            for tb in range(8):
                if tb >= 1:
                    t.wait_ge(p.sem("pa_lev"), (tb - 1) * 4 + 4)
                for dc in range(8):
                    k = tb * 8 + dc
                    t.wait_ge(p.sem("pa_xt"), 16 * (k + 1))
                    t.ldweights(wr_b[:, 8 * dc:8 * (dc + 1)])
                    for q in range(4):
                        ii = t.matmul(ps_l[q][:], wr_b[:, 8 * dc:8 * (dc + 1)],
                                      xnT[k % 2][:, 512 * q:512 * (q + 1)],
                                      start=(dc == 0), stop=(dc == 7))
                    p.inc(ii, "pa_lmm", 1)

            t.wait_ge(p.sem("pa_gini"), 1)
            for gb in range(NBATCH):
                t.wait_ge(p.sem("pa_lgld"), 16 * (gb + 1))
                if gb >= 2:
                    t.wait_ge(p.sem("pa_lgbev"), gb - 1)
                ii = t.transpose(ps_t[gb % 2][:], lgin[gb % 2][:], identf[:])
                p.inc(ii, "pa_l1tr", 1)
                t.wait_ge(p.sem("pa_indrdy"), gb + 1)
                if gb >= 2:
                    t.wait_ge(p.sem("pa_indev"), gb - 1)
                ii = t.transpose(ps_i[gb % 2][:], indb[:], identf[:])
                p.inc(ii, "pa_i1tr", 1)
            for gb in range(NBATCH):
                t.wait_ge(p.sem("pa_exld"), 16 * (gb + 1))
                if gb >= 2:
                    t.wait_ge(p.sem("pa_exbev"), gb - 1)
                ii = t.transpose(ps_t[gb % 2][:], exin[gb % 2][:], identf[:])
                p.inc(ii, "pa_l2tr", 1)


def build_prep():
    p = Prog()
    nc = p.nc
    x = nc.dram_tensor("x", [NTOK, D], F32, kind="ExternalInput").ap()
    wr = nc.dram_tensor("w_router", [D, EXP], F32, kind="ExternalInput").ap()
    selmask = nc.dram_tensor("selmask", [128, 2048], F32, kind="ExternalInput").ap()
    tokid_pc = nc.dram_tensor("tokid_pc", [128, 16], F32, kind="ExternalInput").ap()
    xn_all = nc.dram_tensor("xn_all", [NTOK, D], BF16, kind="ExternalOutput").ap()
    routing = nc.dram_tensor("routing", [128, 512], F32, kind="ExternalOutput").ap()
    inv_out = nc.dram_tensor("inv", [CAP * EXP, 1], F32, kind="ExternalOutput").ap()
    inv_part = nc.dram_tensor("inv_part", [CAP * EXP, 1], F32, kind="Internal").ap()
    inv_red = nc.dram_tensor("inv_red", [CAP * EXP, 1], F32, kind="Internal").ap()
    emit_prep(p, x, wr, xn_all, routing, inv_out, inv_part, inv_red, selmask, tokid_pc)
    p.close()
    return nc


def prep_host_inputs(x_np, wr_np):
    maps = []
    for c in range(NCORES):
        selmask = np.zeros((128, 2048), np.float32)
        for jj in range(16):
            selmask[:, 128 * jj + 16 * c + jj] = 1.0
        tokid_pc = (128 * (16 * c + np.arange(16))[None, :]
                    + np.arange(128)[:, None]).astype(np.float32)
        maps.append({"x": x_np, "w_router": wr_np, "selmask": selmask,
                     "tokid_pc": tokid_pc})
    return maps



def emit_moe(p, xn_all, inv_pc, w1_in, w2_in, y_out):
    nc = p.nc
    w1_bf = nc.dram_tensor("mo_w1bf", [D, 4096], BF16, kind="Internal").ap()
    w2_bf = nc.dram_tensor("mo_w2bf", [4096, D], BF16, kind="Internal").ap()

    invsb = p.sb("mo_inv", [128, 32])
    invib = [p.sb(f"mo_invi{i}", [128, 1], I32) for i in range(32)]
    xb_tok = [p.sb(f"mo_xbt{i}", [128, D], BF16) for i in range(2)]
    xbT = p.sb("mo_xbT", [128, 8 * 1024], BF16)
    w1c = [p.sb(f"mo_w1c{i}", [128, 8 * 128], BF16) for i in range(2)]
    w2c = [p.sb(f"mo_w2c{i}", [128, 1024], BF16) for i in range(2)]
    gs1 = [p.sb(f"mo_gs1_{i}", [128, 512]) for i in range(2)]
    gs1b = [p.sb(f"mo_gs1b_{i}", [128, 512]) for i in range(2)]
    gs2 = [p.sb(f"mo_gs2_{i}", [128, 512]) for i in range(2)]
    gs3 = [p.sb(f"mo_gs3_{i}", [128, 512]) for i in range(2)]
    gs4 = [p.sb(f"mo_gs4_{i}", [128, 512]) for i in range(2)]
    hT = p.sb("mo_hT", [128, 32 * 1024], BF16)
    ysb = [p.sb(f"mo_ysb{i}", [128, D], BF16) for i in range(2)]
    identfm = p.sb("mo_identf", [128, 128])
    onesm = p.sb("mo_ones", [128, 128])
    ident_bf = p.sb("mo_identb", [128, 128], BF16)
    ps_h = [p.ps(f"mo_psh{i}", [128, 512]) for i in range(4)]
    ps_x = [p.ps(f"mo_psx{i}", [128, 128], BF16) for i in range(2)]
    ps_y = [p.ps(f"mo_psy{i}", [128, 512]) for i in range(2)]

    NSB = 4

    def hbank(g, q):
        return ps_h[(g % 2) * 2 + q]

    with nc.Block() as blk:
        @blk.gpsimd
        def _(g):
            g.wait_ge(p.sem("mo_vini"), 1)
            ii = g.affine_select(identfm[:], onesm[:], [[-1, 128]], ALU.is_equal, 0.0,
                                 base=0, channel_multiplier=1)
            p.inc(ii, "mo_idaf", 1)
            ii = g.dma_start(out=w1_bf, in_=w1_in)
            p.inc(ii, "mo_cast", 16)
            ii = g.dma_start(out=w2_bf, in_=w2_in)
            p.inc(ii, "mo_cast", 16)
            for sb in range(NSB):
                for st8 in range(8):
                    st = sb * 8 + st8
                    g.wait_ge(p.sem("mo_invi"), st + 1)
                    if st >= 2:
                        g.wait_ge(p.sem("mo_xtr"), 8 * (st - 1))
                    ii = g.indirect_dma_start(
                        xb_tok[st % 2][:], None, xn_all,
                        bass.IndirectOffsetOnAxis(invib[st][:], 0),
                        bounds_check=NTOK - 1, oob_is_err=False)
                    p.inc(ii, "mo_gat", 16)
                for fc in range(32):
                    gg = sb * 32 + fc
                    for q in range(2):
                        k = 2 * gg + q
                        g.wait_ge(p.sem("mo_gsq"), k + 1)
                        if k >= 2:
                            g.wait_ge(p.sem("mo_gt"), k - 1)
                        ii = g.tensor_scalar(gs1b[k % 2][:], gs1[k % 2][:], 0.044715,
                                             1.0, ALU.mult, ALU.add)
                        p.inc(ii, "mo_gp1", 1)
                        g.wait_ge(p.sem("mo_gth"), k + 1)
                        if k >= 2:
                            g.wait_ge(p.sem("mo_hev"), k - 1)
                        ii = g.tensor_scalar(gs4[k % 2][:], gs3[k % 2][:], 0.5, 0.5,
                                             ALU.mult, ALU.add)
                        p.inc(ii, "mo_gp2", 1)
                for st8 in range(8):
                    kk = sb * 8 + st8
                    g.wait_ge(p.sem("mo_yev"), 2 * kk + 2)
                    ii = g.dma_start(out=y_out[128 * kk:128 * (kk + 1), :],
                                     in_=ysb[st8 % 2][:])
                    p.inc(ii, "mo_yst", 16)
            g.wait_ge(p.sem("mo_yst"), 16 * 32)

        @blk.sync
        def _(sync):
            ii = sync.dma_start(out=invsb[:], in_=inv_pc)
            p.inc(ii, "mo_invld", 16)
            sync.wait_ge(p.sem("mo_cast"), 32)
            for sb in range(NSB):
                for fc in range(32):
                    k = sb * 32 + fc
                    if k >= 2:
                        sync.wait_ge(p.sem("mo_hmm"), k - 1)
                    ii = sync.dma_start(
                        out=w1c[k % 2][:].rearrange("p (c f) -> p c f", c=8),
                        in_=w1_bf[:, 128 * fc:128 * (fc + 1)]
                            .rearrange("(c p) f -> p c f", p=128))
                    p.inc(ii, "mo_w1ld", 16)
                for st8 in range(8):
                    for fc in range(32):
                        k = (sb * 8 + st8) * 32 + fc
                        if k >= 2:
                            sync.wait_ge(p.sem("mo_ymm"), k - 1)
                        ii = sync.dma_start(out=w2c[k % 2][:],
                                            in_=w2_bf[128 * fc:128 * (fc + 1), :])
                        p.inc(ii, "mo_w2ld", 16)

        @blk.vector
        def _(v):
            v.memset(onesm[:], 1.0)
            ii = v.memset(identfm[:], 0.0)
            p.inc(ii, "mo_vini", 1)
            v.wait_ge(p.sem("mo_idaf"), 1)
            ii = v.tensor_copy(ident_bf[:], identfm[:])
            p.inc(ii, "mo_idb", 1)
            v.wait_ge(p.sem("mo_invld"), 16)
            for st in range(32):
                ii = v.tensor_copy(invib[st][:], invsb[:, st:st + 1])
                p.inc(ii, "mo_invi", 1)
            for sb in range(NSB):
                for st8 in range(8):
                    stg = sb * 8 + st8
                    for dc in range(8):
                        kk = stg * 8 + dc
                        v.wait_ge(p.sem("mo_xtr"), kk + 1)
                        ii = v.tensor_copy(
                            xbT[:, 1024 * dc + 128 * st8:1024 * dc + 128 * (st8 + 1)],
                            ps_x[kk % 2][:])
                        p.inc(ii, "mo_xev", 1)
                for fc in range(32):
                    gg = sb * 32 + fc
                    for q in range(2):
                        k = 2 * gg + q
                        v.wait_ge(p.sem("mo_gp1"), k + 1)
                        if k >= 2:
                            v.wait_ge(p.sem("mo_gth"), k - 1)
                        ii = v.scalar_tensor_tensor(gs2[k % 2][:], gs1b[k % 2][:],
                                                    0.7978845608028654,
                                                    hbank(gg, q)[:], ALU.mult, ALU.mult)
                        p.inc(ii, "mo_gt", 1)
                        v.wait_ge(p.sem("mo_gp2"), k + 1)
                        v.wait_ge(p.sem("mo_ymm"), 256 * sb)
                        ii = v.tensor_tensor(
                            hT[:, 1024 * fc + 512 * q:1024 * fc + 512 * (q + 1)],
                            gs4[k % 2][:], hbank(gg, q)[:], ALU.mult)
                        p.inc(ii, "mo_hev", 1)
                for st8 in range(8):
                    kk = sb * 8 + st8
                    for dh in range(2):
                        v.wait_ge(p.sem("mo_ymm"), kk * 32 + 32)
                        if kk >= 2:
                            v.wait_ge(p.sem("mo_yst"), 16 * (kk - 1))
                        ii = v.tensor_copy(ysb[st8 % 2][:, 512 * dh:512 * (dh + 1)],
                                           ps_y[(2 * kk + dh) % 2][:])
                        p.inc(ii, "mo_yev", 1)

        @blk.scalar
        def _(s):
            for sb in range(NSB):
                for fc in range(32):
                    gg = sb * 32 + fc
                    for q in range(2):
                        k = 2 * gg + q
                        s.wait_ge(p.sem("mo_hmm"), gg + 1)
                        if k >= 2:
                            s.wait_ge(p.sem("mo_gp1"), k - 1)
                        ii = s.activation(gs1[k % 2][:], hbank(gg, q)[:], AF.Square)
                        p.inc(ii, "mo_gsq", 1)
                        s.wait_ge(p.sem("mo_gt"), k + 1)
                        if k >= 2:
                            s.wait_ge(p.sem("mo_gp2"), k - 1)
                        ii = s.activation(gs3[k % 2][:], gs2[k % 2][:], AF.Tanh)
                        p.inc(ii, "mo_gth", 1)

        @blk.tensor
        def _(t):
            t.wait_ge(p.sem("mo_vini"), 1)
            for sb in range(NSB):
                for st8 in range(8):
                    stg = sb * 8 + st8
                    t.wait_ge(p.sem("mo_gat"), 16 * (stg + 1))
                    t.wait_ge(p.sem("mo_idb"), 1)
                    if sb >= 1:
                        t.wait_ge(p.sem("mo_hmm"), 32 * sb)
                    for dc in range(8):
                        kk = stg * 8 + dc
                        if kk >= 2:
                            t.wait_ge(p.sem("mo_xev"), kk - 1)
                        ii = t.transpose(ps_x[kk % 2][:],
                                         xb_tok[st8 % 2][:, 128 * dc:128 * (dc + 1)],
                                         ident_bf[:])
                        p.inc(ii, "mo_xtr", 1)
                for fc in range(32):
                    gg = sb * 32 + fc
                    t.wait_ge(p.sem("mo_xev"), 64 * (sb + 1))
                    t.wait_ge(p.sem("mo_w1ld"), 16 * (gg + 1))
                    if gg >= 2:
                        t.wait_ge(p.sem("mo_hev"), 2 * (gg - 1))
                    for dc in range(8):
                        t.ldweights(w1c[gg % 2][:, 128 * dc:128 * (dc + 1)])
                        for q in range(2):
                            ii = t.matmul(hbank(gg, q)[:],
                                          w1c[gg % 2][:, 128 * dc:128 * (dc + 1)],
                                          xbT[:, 1024 * dc + 512 * q:1024 * dc + 512 * (q + 1)],
                                          start=(dc == 0), stop=(dc == 7))
                    p.inc(ii, "mo_hmm", 1)
                for st8 in range(8):
                    kk = sb * 8 + st8
                    t.wait_ge(p.sem("mo_hev"), 64 * (sb + 1))
                    if kk >= 1:
                        t.wait_ge(p.sem("mo_yev"), 2 * kk)
                    for fc in range(32):
                        k = kk * 32 + fc
                        t.wait_ge(p.sem("mo_w2ld"), 16 * (k + 1))
                        t.ldweights(hT[:, 1024 * fc + 128 * st8:1024 * fc + 128 * (st8 + 1)])
                        for dh in range(2):
                            ii = t.matmul(ps_y[(2 * kk + dh) % 2][:],
                                          hT[:, 1024 * fc + 128 * st8:1024 * fc + 128 * (st8 + 1)],
                                          w2c[k % 2][:, 512 * dh:512 * (dh + 1)],
                                          start=(fc == 0), stop=(fc == 31))
                        p.inc(ii, "mo_ymm", 1)

    return p


def build_moe():
    p = Prog()
    nc = p.nc
    xn_all = nc.dram_tensor("xn_all", [NTOK, D], BF16, kind="ExternalInput").ap()
    inv_pc = nc.dram_tensor("inv_pc", [128, 32], F32, kind="ExternalInput").ap()
    w1_in = nc.dram_tensor("w1", [D, 4096], F32, kind="ExternalInput").ap()
    w2_in = nc.dram_tensor("w2", [4096, D], F32, kind="ExternalInput").ap()
    y_out = nc.dram_tensor("y", [CAP, D], BF16, kind="ExternalOutput").ap()
    emit_moe(p, xn_all, inv_pc, w1_in, w2_in, y_out)
    p.close()
    return nc



CSHIFT = 16.0
ABLATE = set()


def emit_mla(p, xnT_in, hsel, wts, s_out, nheads=4, do_attn=True, nqb=16, ntb=16, do_epi=True):
    nc = p.nc
    (wdq_t, wuq_t, wdkv_t, wukv_t, wqr_t, wkr_t, wo_t, qnw, knw) = wts
    S = Sched(p)

    hselib = [p.sb(f"ml_hsel{i}", [64, 1], I32) for i in range(4)]
    hselsb = p.sb("ml_hself", [64, 4])
    xh_sb = [p.sb(f"ml_xh{i}", [64, 8192], BF16) for i in range(2)]
    wdq = p.sb("ml_wdq", [64, 512], BF16)
    wuq = p.sb("ml_wuq", [128, 4 * 64], BF16)
    wdkv = p.sb("ml_wdkv", [64, 1024], BF16)
    wukv = p.sb("ml_wukv", [128, 8 * 128], BF16)
    wqr = p.sb("ml_wqr", [64, 64])
    wqrb = p.sb("ml_wqrb", [64, 64], BF16)
    wkrb = p.sb("ml_wkrb", [64, 64], BF16)
    wob = p.sb("ml_wob", [64, 64], BF16)
    qnws = p.sb("ml_qnw", [64, 1])
    knws = p.sb("ml_knw", [64, 1])
    wqk = p.sb("ml_wqk", [64, 1])
    wf32 = p.sb("ml_wf32", [128, 1920])
    onesf1 = p.sb("ml_onesf1", [64, 1], BF16)
    onesb1 = p.sb("ml_onesb1", [1, 64], BF16)
    zer1024 = p.sb("ml_zer1024", [128, 1024])
    ones64 = p.sb("ml_ones64", [64, 64])
    identf64 = p.sb("ml_identf64", [64, 64])
    identb64 = p.sb("ml_identb64", [64, 64], BF16)
    cshift = p.sb("ml_cshift", [128, 1])
    cpool = p.sb("ml_cpool", [128, 2])
    qT = p.sb("ml_qT", [128, 8192], BF16)
    kT = p.sb("ml_kT", [128, 8192], BF16)
    v65 = p.sb("ml_v65", [128, 65 * 64], BF16)
    cq_sb = p.sb("ml_cq", [128, 2048], BF16)
    ckv_sb = p.sb("ml_ckv", [128, 4096], BF16)
    qc_f = p.sb("ml_qcf", [64, 512])
    qc_b = p.sb("ml_qcb", [64, 512], BF16)
    kc_f = p.sb("ml_kcf", [64, 512])
    sqs = p.sb("ml_sqs", [64, 512], BF16)
    vb = p.sb("ml_vb", [64, 512], BF16)
    ctxb = p.sb("ml_ctxb", [64, 512], BF16)
    r1 = p.sb("ml_r1", [1, 512])
    r1b = p.sb("ml_r1b", [1, 512], BF16)
    bc_f = p.sb("ml_bcf", [64, 512])
    att = [p.sb(f"ml_att{i}", [128, 512], BF16) for i in range(2)]
    msks = p.sb("ml_msks", [128, 1024])
    mskscr = p.sb("ml_mskscr", [128, 512])
    recz = p.sb("ml_recz", [1, 512])
    s_fin = p.sb("ml_sfin", [64, 8192], BF16)

    ps_a = [p.ps(f"ml_psa{i}", [128, 512]) for i in range(2)]
    ps_q = p.ps("ml_psq", [64, 512])
    ps_q2 = p.ps("ml_psq2", [64, 512])
    ps_bc = p.ps("ml_psbc", [64, 512])
    ps_kv = p.ps("ml_pskv", [128, 512])
    ps_vt = p.ps("ml_psvt", [128, 64], BF16)
    ps_c = p.ps("ml_psc", [65, 512])

    A = S.add
    A("vector", lambda e: e.memset(zer1024[:], 0.0), w=[zer1024])
    A("vector", lambda e: e.memset(ones64[:], 1.0), w=[ones64])
    A("vector", lambda e: e.memset(onesf1[:], 1.0), w=[onesf1])
    A("vector", lambda e: e.memset(onesb1[:], 1.0), w=[onesb1])
    A("vector", lambda e: e.memset(v65[:], 1.0), w=[(v65, "init")])
    A("vector", lambda e: e.memset(cshift[:], -CSHIFT), w=[cshift])
    A("vector", lambda e: e.memset(cpool[:], 0.0), w=[cpool])
    A("vector", lambda e: e.memset(cpool[:, 0:1], RMS_EPS), w=[cpool])
    A("gpsimd", lambda e: e.affine_select(msks[:], zer1024[:], [[1, 1024]],
                                          ALU.is_ge, -1e9, base=-512,
                                          channel_multiplier=-1),
      r=[zer1024], w=[msks])
    A("gpsimd", lambda e: e.affine_select(identf64[:], ones64[:], [[-1, 64]],
                                          ALU.is_equal, 0.0, base=0,
                                          channel_multiplier=1),
      r=[ones64], w=[identf64])
    A("vector", lambda e: e.tensor_copy(identb64[:], identf64[:]),
      r=[identf64], w=[identb64])
    A("sync", lambda e: e.dma_start(out=hselsb[:], in_=hsel), w=[hselsb], dma=True)
    A("sync", lambda e: e.dma_start(out=wf32[0:64, 0:512], in_=wdq_t),
      w=[(wf32, "dq")], dma=True)
    A("sync", lambda e: e.dma_start(
        out=wf32[0:128, 512:768].rearrange("p (c f) -> p c f", c=4),
        in_=wuq_t.rearrange("(c p) f -> p c f", p=128)), w=[(wf32, "uq")], dma=True)
    A("sync", lambda e: e.dma_start(out=wf32[0:64, 768:1792], in_=wdkv_t),
      w=[(wf32, "dkv")], dma=True)
    A("sync", lambda e: e.dma_start(
        out=wf32[0:128, 1792 - 1024:1792].rearrange("p (c f) -> p c f", c=8)
        if False else wf32[0:128, 0:0], in_=wukv_t) if False else
      e_none(), dma=True) if False else None
    A("sync", lambda e: e.dma_start(out=wqr[:], in_=wqr_t), w=[wqr], dma=True)
    A("sync", lambda e: e.dma_start(out=wf32[0:64, 1792:1856], in_=wkr_t),
      w=[(wf32, "kr")], dma=True)
    A("sync", lambda e: e.dma_start(out=wf32[0:64, 1856:1920], in_=wo_t),
      w=[(wf32, "wo")], dma=True)
    A("sync", lambda e: e.dma_start(out=qnws[:], in_=qnw), w=[qnws], dma=True)
    A("sync", lambda e: e.dma_start(out=knws[:], in_=knw), w=[knws], dma=True)
    for _hi in range(4):
        A("vector", lambda e, _hi=_hi: e.tensor_copy(hselib[_hi][:],
                                                     hselsb[:, _hi:_hi + 1]),
          r=[hselsb], w=[hselib[_hi]])
    A("vector", lambda e: e.tensor_copy(wdq[:], wf32[0:64, 0:512]),
      r=[(wf32, "dq")], w=[wdq])
    A("vector", lambda e: e.tensor_copy(wuq[:], wf32[0:128, 512:768]),
      r=[(wf32, "uq")], w=[wuq])
    A("vector", lambda e: e.tensor_copy(wdkv[:], wf32[0:64, 768:1792]),
      r=[(wf32, "dkv")], w=[wdkv])
    A("vector", lambda e: e.tensor_copy(wkrb[:], wf32[0:64, 1792:1856]),
      r=[(wf32, "kr")], w=[wkrb])
    A("vector", lambda e: e.tensor_copy(wob[:], wf32[0:64, 1856:1920]),
      r=[(wf32, "wo")], w=[wob])
    A("vector", lambda e: e.tensor_scalar(wqr[:], wqr[:], qnws[:], None, ALU.mult),
      r=[wqr, qnws], w=[wqr])
    A("vector", lambda e: e.tensor_copy(wqrb[:], wqr[:]), r=[wqr], w=[wqrb])
    A("vector", lambda e: e.tensor_tensor(wqk[:], qnws[:], knws[:], ALU.mult),
      r=[qnws, knws], w=[wqk])
    A("sync", lambda e: e.dma_start(
        out=wf32[0:128, 0:1024].rearrange("p (c f) -> p c f", c=8),
        in_=wukv_t.rearrange("(c p) f -> p c f", p=128)),
      r=[wdq, wuq, wdkv, (wf32, "dq"), (wf32, "uq"), (wf32, "dkv")],
      w=[(wf32, "ukv")], dma=True)
    A("vector", lambda e: e.tensor_copy(wukv[:], wf32[0:128, 0:1024]),
      r=[(wf32, "ukv")], w=[wukv])

    for hd in range(nheads):
        g = hd % 2
        xrow = 0
        xh = xh_sb[g]
        A("gpsimd", lambda e, g=g, hd=hd: e.indirect_dma_start(
            xh_sb[g][:], None, xnT_in,
            bass.IndirectOffsetOnAxis(hselib[hd][:], 0),
            bounds_check=2047, oob_is_err=False),
          r=[hselib[hd]], w=[xh_sb[g]], dma=True)

        def xs(c0, n=512, xh=xh, xrow=xrow):
            return xh[xrow:xrow + 64, c0:c0 + n]

        for tb in range(ntb):
            c0 = 512 * tb

            def mmchain(psum, lhs_full, rhs_full, nchunks, csz, e=None):
                pass

            for l in range(4):
                A("tensor", lambda e, l=l, c0=c0, xh=xh, xrow=xrow:
                  e.matmul(ps_a[l % 2][:], wdq[:, 128 * l:128 * (l + 1)],
                           xh[xrow:xrow + 64, c0:c0 + 512], start=True, stop=True),
                  r=[wdq, (xh, "")], w=[ps_a[l % 2]])
                A("vector", lambda e, l=l: e.tensor_copy(
                    cq_sb[:, 512 * l:512 * (l + 1)], ps_a[l % 2][:]),
                  r=[ps_a[l % 2]], w=[(cq_sb, f"{l}")])

            def qcmm(e):
                for l in range(4):
                    ii = e.matmul(ps_q[:], wuq[:, 64 * l:64 * (l + 1)],
                                  cq_sb[:, 512 * l:512 * (l + 1)],
                                  start=(l == 0), stop=(l == 3))
                return ii
            A("tensor", qcmm, r=[wuq] + [(cq_sb, f"{l}") for l in range(4)], w=[ps_q])
            A("vector", lambda e: e.tensor_copy(qc_f[:], ps_q[:]), r=[ps_q], w=[qc_f])
            if "qcb" not in ABLATE:
                A("scalar", lambda e: e.activation(qc_b[:], ps_q[:], AF.Identity),
                  r=[ps_q], w=[qc_b])
            if "sq1" not in ABLATE:
                A("scalar", lambda e: e.activation(sqs[:], qc_f[:], AF.Square),
                  r=[qc_f], w=[sqs])
            if "msmm1" not in ABLATE:
                A("tensor", lambda e: e.matmul(ps_bc[0:1, :], onesf1[:], sqs[:],
                                               start=True, stop=True),
                  r=[onesf1, sqs], w=[ps_bc])
                A("vector", lambda e: e.tensor_scalar(r1[:], ps_bc[0:1, :], 1.0 / 64.0,
                                                      None, ALU.mult),
                  r=[ps_bc], w=[r1])
            A("vector", lambda e: e.tensor_scalar(r1[:], r1[:], cpool[0:1, 0:1],
                                                  None, ALU.add),
              r=[r1, cpool], w=[r1])
            A("vector", lambda e: e.reciprocal(r1[:], r1[:]), r=[r1], w=[r1])
            if "sqrt1" not in ABLATE:
                A("scalar", lambda e: e.activation(r1[:], r1[:], AF.Sqrt),
                  r=[r1], w=[r1])
                A("vector", lambda e: e.tensor_scalar(r1[:], r1[:], 0.125, None,
                                                      ALU.mult), r=[r1], w=[r1])
            A("vector", lambda e: e.tensor_copy(r1b[:], r1[:]), r=[r1], w=[r1b])
            A("tensor", lambda e: e.matmul(ps_bc[:], onesb1[:], r1b[:],
                                           start=True, stop=True),
              r=[onesb1, r1b], w=[ps_bc])
            A("vector", lambda e: e.tensor_copy(bc_f[:], ps_bc[:]),
              r=[ps_bc], w=[bc_f])
            A("tensor", lambda e: e.matmul(ps_q2[:], wqrb[:], qc_b[:],
                                           start=True, stop=True),
              r=[wqrb, qc_b], w=[ps_q2])
            A("vector", lambda e, c0=c0: e.tensor_tensor(
                qT[0:64, c0:c0 + 512], qc_f[:], bc_f[:], ALU.mult),
              r=[qc_f, bc_f], w=[(qT, f"a{tb}")])
            A("vector", lambda e, c0=c0: e.tensor_tensor(
                qT[64:128, c0:c0 + 512], ps_q2[:], bc_f[:], ALU.mult),
              r=[ps_q2, bc_f], w=[(qT, f"b{tb}")])
            for l in range(8):
                A("tensor", lambda e, l=l, c0=c0, xh=xh, xrow=xrow:
                  e.matmul(ps_a[l % 2][:], wdkv[:, 128 * l:128 * (l + 1)],
                           xh[xrow:xrow + 64, c0:c0 + 512], start=True, stop=True),
                  r=[wdkv, (xh, "")], w=[ps_a[l % 2]])
                A("vector", lambda e, l=l: e.tensor_copy(
                    ckv_sb[:, 512 * l:512 * (l + 1)], ps_a[l % 2][:]),
                  r=[ps_a[l % 2]], w=[(ckv_sb, f"{l}")])

            def kvmm(e):
                for l in range(8):
                    ii = e.matmul(ps_kv[:], wukv[:, 128 * l:128 * (l + 1)],
                                  ckv_sb[:, 512 * l:512 * (l + 1)],
                                  start=(l == 0), stop=(l == 7))
                return ii
            A("tensor", kvmm, r=[wukv] + [(ckv_sb, f"{l}") for l in range(8)],
              w=[ps_kv])
            A("vector", lambda e: e.tensor_copy(kc_f[:], ps_kv[0:64, :]),
              r=[ps_kv], w=[kc_f])
            A("scalar", lambda e: e.activation(vb[:], ps_kv[64:128, :], AF.Identity),
              r=[ps_kv], w=[vb])
            A("scalar", lambda e: e.activation(sqs[:], kc_f[:], AF.Square),
              r=[kc_f], w=[sqs])
            if "msmm1" not in ABLATE:
                A("tensor", lambda e: e.matmul(ps_bc[0:1, :], onesf1[:], sqs[:],
                                               start=True, stop=True),
                  r=[onesf1, sqs], w=[ps_bc])
                A("vector", lambda e: e.tensor_scalar(r1[:], ps_bc[0:1, :], 1.0 / 64.0,
                                                      None, ALU.mult),
                  r=[ps_bc], w=[r1])
            A("vector", lambda e: e.tensor_scalar(r1[:], r1[:], cpool[0:1, 0:1],
                                                  None, ALU.add),
              r=[r1, cpool], w=[r1])
            A("vector", lambda e: e.reciprocal(r1[:], r1[:]), r=[r1], w=[r1])
            A("scalar", lambda e: e.activation(r1[:], r1[:], AF.Sqrt),
              r=[r1], w=[r1])
            A("vector", lambda e: e.tensor_copy(r1b[:], r1[:]), r=[r1], w=[r1b])
            A("tensor", lambda e: e.matmul(ps_bc[:], onesb1[:], r1b[:],
                                           start=True, stop=True),
              r=[onesb1, r1b], w=[ps_bc])
            A("vector", lambda e: e.tensor_copy(bc_f[:], ps_bc[:]),
              r=[ps_bc], w=[bc_f])
            A("vector", lambda e, c0=c0: e.scalar_tensor_tensor(
                kT[0:64, c0:c0 + 512], kc_f[:], wqk[:], bc_f[:],
                ALU.mult, ALU.mult),
              r=[kc_f, wqk, bc_f], w=[(kT, f"a{tb}")])
            A("tensor", lambda e, c0=c0, xh=xh, xrow=xrow:
              e.matmul(ps_q2[:], wkrb[:], xh[xrow:xrow + 64, c0:c0 + 512],
                       start=True, stop=True),
              r=[wkrb, (xh, "")], w=[ps_q2])
            A("scalar", lambda e, c0=c0: e.activation(
                kT[64:128, c0:c0 + 512], ps_q2[:], AF.Identity),
              r=[ps_q2], w=[(kT, f"b{tb}")])
            for l in range(4):
                kt_idx = 4 * tb + l
                A("tensor", lambda e, l=l: e.transpose(
                    ps_vt[:], vb[:, 128 * l:128 * (l + 1)], identb64[:]),
                  r=[vb, identb64], w=[ps_vt])
                A("vector", lambda e, kt_idx=kt_idx: e.tensor_copy(
                    v65[:, 65 * kt_idx:65 * kt_idx + 64], ps_vt[:]),
                  r=[ps_vt], w=[(v65, f"{kt_idx}")])

        for qb in range(nqb if do_attn else 0):
            for kb in range(qb + 1):
                for j in range(4):
                    kt_idx = 4 * kb + j
                    A("tensor", lambda e, qb=qb, kb=kb, j=j: e.matmul(
                        ps_a[j % 2][:],
                        kT[:, 512 * kb + 128 * j:512 * kb + 128 * (j + 1)],
                        qT[:, 512 * qb:512 * (qb + 1)], start=True, stop=True),
                      r=[(kT, f"a{kb}"), (kT, f"b{kb}"), (qT, f"a{qb}"),
                         (qT, f"b{qb}")],
                      w=[ps_a[j % 2]])
                    if kb == qb:
                        A("vector", lambda e, j=j: e.tensor_tensor(
                            mskscr[:], ps_a[j % 2][:],
                            msks[:, 512 - 128 * j:1024 - 128 * j], ALU.add),
                          r=[ps_a[j % 2], msks], w=[mskscr])
                        A("scalar", lambda e, j=j: e.activation(
                            att[j % 2][:], mskscr[:], AF.Exp, bias=cshift[:]),
                          r=[mskscr, cshift], w=[att[j % 2]])
                    else:
                        A("scalar", lambda e, j=j: e.activation(
                            att[j % 2][:], ps_a[j % 2][:], AF.Exp, bias=cshift[:]),
                          r=[ps_a[j % 2], cshift], w=[att[j % 2]])
                    A("tensor", lambda e, kt_idx=kt_idx, j=j, qb=qb, kb=kb: e.matmul(
                        ps_c[:], v65[:, 65 * kt_idx:65 * kt_idx + 65],
                        att[j % 2][:], start=(kb == 0 and j == 0),
                        stop=(kb == qb and j == 3)),
                      r=[(v65, f"{kt_idx}"), (v65, "init"), att[j % 2]], w=[ps_c])
            if not do_epi:
                continue
            A("vector", lambda e: e.reciprocal(recz[:], ps_c[64:65, :]),
              r=[ps_c], w=[recz])
            A("vector", lambda e: e.tensor_copy(r1b[:], recz[:]), r=[recz], w=[r1b])
            A("scalar", lambda e: e.activation(ctxb[:], ps_c[0:64, :], AF.Identity),
              r=[ps_c], w=[ctxb])
            A("tensor", lambda e: e.matmul(ps_bc[:], onesb1[:], r1b[:],
                                           start=True, stop=True),
              r=[onesb1, r1b], w=[ps_bc])
            A("vector", lambda e: e.tensor_copy(bc_f[:], ps_bc[:]),
              r=[ps_bc], w=[bc_f])
            A("tensor", lambda e: e.matmul(ps_q2[:], wob[:], ctxb[:],
                                           start=True, stop=True),
              r=[wob, ctxb], w=[ps_q2])
            A("vector", lambda e, qb=qb: e.tensor_tensor(
                s_fin[:, 512 * qb:512 * (qb + 1)], ps_q2[:], bc_f[:], ALU.mult),
              r=[ps_q2, bc_f], w=[(s_fin, f"{qb}")])
        if not (do_attn and do_epi):
            continue
        A("sync", lambda e, hd=hd: e.dma_start(
            out=s_out[64 * hd:64 * (hd + 1), :], in_=s_fin[:]),
          r=[(s_fin, f"{qb}") for qb in range(16)], w=[(s_fin, "stored")], dma=True)

    S.prepare()
    with nc.Block() as blk:
        S.emit(blk)
    return p


def build_mla(nheads=4, do_attn=True, nqb=16, ntb=16, do_epi=True):
    p = Prog()
    nc = p.nc
    xnT_in = nc.dram_tensor("xnT", [2048, 8192], BF16, kind="ExternalInput").ap()
    hsel = nc.dram_tensor("hsel", [64, 4], F32, kind="ExternalInput").ap()
    names = ["wdq_t", "wuq_t", "wdkv_t", "wukv_t", "wqr_t", "wkr_t", "wo_t"]
    shapes = [[64, 512], [512, 64], [64, 1024], [1024, 128], [64, 64], [64, 64],
              [64, 64]]
    wts = [nc.dram_tensor(n, s, F32, kind="ExternalInput").ap()
           for n, s in zip(names, shapes)]
    wts.append(nc.dram_tensor("qnw", [64, 1], F32, kind="ExternalInput").ap())
    wts.append(nc.dram_tensor("knw", [64, 1], F32, kind="ExternalInput").ap())
    s_out = nc.dram_tensor("s_coreT", [256, 8192], BF16, kind="ExternalOutput").ap()
    emit_mla(p, xnT_in, hsel, tuple(wts), s_out, nheads, do_attn, nqb, ntb, do_epi)
    p.close()
    return nc


def mla_host_inputs(xn_all_bf, weights):
    xnT = np.ascontiguousarray(
        np.asarray(xn_all_bf).reshape(2, 8192, 1024).transpose(2, 0, 1)
        .reshape(2048, 8192))
    maps = []
    for c in range(NCORES):
        hs = np.zeros((64, 4), np.float32)
        for i in range(4):
            j = 4 * c + i
            b, h = j // 16, j % 16
            for d in range(64):
                hs[d, i] = 2 * (64 * h + d) + b
        m = {"xnT": xnT, "hsel": hs,
             "wdq_t": np.ascontiguousarray(weights["W_DQ"].T),
             "wuq_t": np.ascontiguousarray(weights["W_UQ"].T),
             "wdkv_t": np.ascontiguousarray(weights["W_DKV"].T),
             "wukv_t": np.ascontiguousarray(weights["W_UKV"].T),
             "wqr_t": np.ascontiguousarray(weights["W_QR"].T),
             "wkr_t": np.ascontiguousarray(weights["W_KR"].T),
             "wo_t": np.ascontiguousarray(weights["W_O"].T),
             "qnw": weights["q_norm_w"].reshape(64, 1),
             "knw": weights["k_norm_w"].reshape(64, 1)}
        maps.append(m)
    return maps



def emit_combine(p, x_rows, s_rows, y_all, rout_pc, bias_in, x1_out, xn1_out,
                 xn1f_out):
    nc = p.nc
    S = Sched(p)
    A = S.add
    NT = TPC // 128

    routsb = p.sb("cb_rout", [128, 64])
    slot_i = [[p.sb(f"cb_si{i}_{k}", [128, 1], I32) for k in range(2)] for i in range(2)]
    ybuf = [[p.sb(f"cb_y{i}_{k}", [128, D], BF16) for k in range(2)] for i in range(2)]
    xbuf = [p.sb(f"cb_x{i}", [128, D]) for i in range(2)]
    sbuf = [p.sb(f"cb_s{i}", [128, D], BF16) for i in range(2)]
    gbuf = p.sb("cb_g", [128, D])
    g2buf = p.sb("cb_g2", [128, D])
    sgbuf = p.sb("cb_sg", [128, D])
    x1t = [p.sb(f"cb_x1t{i}", [128, D]) for i in range(2)]
    xn1t = [p.sb(f"cb_xn1t{i}", [128, D], BF16) for i in range(2)]
    xn1ft = [p.sb(f"cb_xn1ft{i}", [128, D]) for i in range(2)]
    sqscr = p.sb("cb_sqscr", [128, D])
    stt_ = p.sb("cb_stat", [128, 8])
    biasf = p.sb("cb_biasf", [1, 1024])
    biasb = p.sb("cb_biasb", [1, 1024], BF16)
    onesb1 = p.sb("cb_ones1", [1, 128], BF16)
    bias_bc = p.sb("cb_biasbc", [128, 1024])
    cpool = p.sb("cb_cpool", [128, 2])
    ps_b = [p.ps(f"cb_psb{i}", [128, 512]) for i in range(2)]

    A("vector", lambda e: e.memset(onesb1[:], 1.0), w=[onesb1])
    for i in range(2):
        for k in range(2):
            A("vector", lambda e, i=i, k=k: e.memset(ybuf[i][k][:], 0.0),
              w=[ybuf[i][k]])
    A("vector", lambda e: e.memset(cpool[:], 0.0), w=[cpool])
    A("vector", lambda e: e.memset(cpool[:, 0:1], LN_EPS), w=[cpool])
    A("sync", lambda e: e.dma_start(out=routsb[:], in_=rout_pc), w=[routsb], dma=True)
    A("sync", lambda e: e.dma_start(out=biasf[:], in_=bias_in), w=[biasf], dma=True)
    A("vector", lambda e: e.tensor_copy(biasb[:], biasf[:]), r=[biasf], w=[biasb])
    for h in range(2):
        A("tensor", lambda e, h=h: e.matmul(ps_b[h][:], onesb1[:],
                                            biasb[:, 512 * h:512 * (h + 1)],
                                            start=True, stop=True),
          r=[onesb1, biasb], w=[ps_b[h]])
        A("vector", lambda e, h=h: e.tensor_copy(bias_bc[:, 512 * h:512 * (h + 1)],
                                                 ps_b[h][:]),
          r=[ps_b[h]], w=[(bias_bc, f"{h}")])

    for jj in range(NT):
        pr = jj % 2
        r0 = 128 * jj
        for k in range(2):
            A("vector", lambda e, jj=jj, k=k, pr=pr: e.tensor_copy(
                slot_i[pr][k][:], routsb[:, 4 * jj + 2 + k:4 * jj + 3 + k]),
              r=[routsb], w=[slot_i[pr][k]])
            A("gpsimd", lambda e, pr=pr, k=k: e.indirect_dma_start(
                ybuf[pr][k][:], None, y_all,
                bass.IndirectOffsetOnAxis(slot_i[pr][k][:], 0),
                bounds_check=CAP * EXP - 1, oob_is_err=False),
              r=[slot_i[pr][k]], w=[ybuf[pr][k]], dma=True)
        A("sync", lambda e, r0=r0, pr=pr: e.dma_start(
            out=xbuf[pr][:], in_=x_rows[r0:r0 + 128, :]), w=[xbuf[pr]], dma=True)
        A("sync", lambda e, r0=r0, pr=pr: e.dma_start(
            out=sbuf[pr][:], in_=s_rows[r0:r0 + 128, :]), w=[sbuf[pr]], dma=True)
        A("vector", lambda e, jj=jj, pr=pr: e.tensor_scalar(
            gbuf[:], ybuf[pr][0][:], routsb[:, 4 * jj:4 * jj + 1], None, ALU.mult),
          r=[ybuf[pr][0], routsb], w=[gbuf])
        A("vector", lambda e, jj=jj, pr=pr: e.scalar_tensor_tensor(
            gbuf[:], ybuf[pr][1][:], routsb[:, 4 * jj + 1:4 * jj + 2], gbuf[:],
            ALU.mult, ALU.add),
          r=[ybuf[pr][1], routsb, gbuf], w=[gbuf])
        A("gpsimd", lambda e: e.tensor_add(g2buf[:], gbuf[:], bias_bc[:]),
          r=[gbuf, (bias_bc, "0"), (bias_bc, "1")], w=[g2buf])
        A("scalar", lambda e: e.activation(g2buf[:], g2buf[:], AF.Silu),
          r=[g2buf], w=[g2buf])
        A("vector", lambda e, pr=pr: e.tensor_tensor(sgbuf[:], sbuf[pr][:],
                                                     g2buf[:], ALU.mult),
          r=[sbuf[pr], g2buf], w=[sgbuf])
        A("vector", lambda e, pr=pr: e.tensor_tensor(x1t[pr][:], xbuf[pr][:],
                                                     sgbuf[:], ALU.add),
          r=[xbuf[pr], sgbuf], w=[x1t[pr]])
        A("sync", lambda e, r0=r0, pr=pr: e.dma_start(
            out=x1_out[r0:r0 + 128, :], in_=x1t[pr][:]),
          r=[x1t[pr]], w=[(x1_out, f"{jj}")], dma=True)
        A("scalar", lambda e, pr=pr: e.activation(sqscr[:], x1t[pr][:], AF.Square,
                                                  accum_out=stt_[:, 0:1]),
          r=[x1t[pr]], w=[sqscr, (stt_, "ssq")])
        A("vector", lambda e, pr=pr: e.reduce_sum(stt_[:, 1:2], x1t[pr][:],
                                                  axis=AX.X),
          r=[x1t[pr]], w=[(stt_, "mu")])
        A("vector", lambda e: e.tensor_scalar(stt_[:, 1:2], stt_[:, 1:2], 1.0 / D,
                                              None, ALU.mult),
          r=[(stt_, "mu")], w=[(stt_, "mu")])
        A("vector", lambda e: e.tensor_tensor(stt_[:, 2:3], stt_[:, 1:2],
                                              stt_[:, 1:2], ALU.mult),
          r=[(stt_, "mu")], w=[(stt_, "mu2")])
        A("vector", lambda e: e.tensor_scalar(stt_[:, 0:1], stt_[:, 0:1], 1.0 / D,
                                              None, ALU.mult),
          r=[(stt_, "ssq")], w=[(stt_, "ssq")])
        A("vector", lambda e: e.tensor_sub(stt_[:, 2:3], stt_[:, 0:1], stt_[:, 2:3]),
          r=[(stt_, "ssq"), (stt_, "mu2")], w=[(stt_, "var")])
        A("vector", lambda e: e.tensor_scalar(stt_[:, 2:3], stt_[:, 2:3],
                                              cpool[:, 0:1], None, ALU.add),
          r=[(stt_, "var"), cpool], w=[(stt_, "var")])
        A("scalar", lambda e: e.activation(stt_[:, 3:4], stt_[:, 2:3], AF.Sqrt),
          r=[(stt_, "var")], w=[(stt_, "std")])
        A("vector", lambda e: e.reciprocal(stt_[:, 4:5], stt_[:, 3:4]),
          r=[(stt_, "std")], w=[(stt_, "rstd")])
        A("vector", lambda e: e.tensor_scalar(stt_[:, 5:6], stt_[:, 1:2],
                                              stt_[:, 4:5], -1.0, ALU.mult, ALU.mult),
          r=[(stt_, "mu"), (stt_, "rstd")], w=[(stt_, "nmr")])
        A("scalar", lambda e, pr=pr: e.activation(
            xn1ft[pr][:], x1t[pr][:], AF.Identity, bias=stt_[:, 5:6],
            scale=stt_[:, 4:5]),
          r=[x1t[pr], (stt_, "nmr"), (stt_, "rstd")], w=[xn1ft[pr]])
        A("vector", lambda e, pr=pr: e.tensor_copy(xn1t[pr][:], xn1ft[pr][:]),
          r=[xn1ft[pr]], w=[xn1t[pr]])
        A("sync", lambda e, r0=r0, pr=pr: e.dma_start(
            out=xn1f_out[r0:r0 + 128, :], in_=xn1ft[pr][:]),
          r=[xn1ft[pr]], w=[(xn1f_out, f"{jj}")], dma=True)
        A("gpsimd", lambda e, r0=r0, pr=pr: e.dma_start(
            out=xn1_out[r0:r0 + 128, :], in_=xn1t[pr][:]),
          r=[xn1t[pr]], w=[(xn1_out, f"{jj}")], dma=True)

    S.prepare()
    with nc.Block() as blk:
        S.emit(blk)
    return p


def build_combine():
    p = Prog()
    nc = p.nc
    x_rows = nc.dram_tensor("x_rows", [TPC, D], F32, kind="ExternalInput").ap()
    s_rows = nc.dram_tensor("s_rows", [TPC, D], BF16, kind="ExternalInput").ap()
    y_all = nc.dram_tensor("y_all", [CAP * EXP, D], BF16, kind="ExternalInput").ap()
    rout_pc = nc.dram_tensor("rout_pc", [128, 64], F32, kind="ExternalInput").ap()
    bias_in = nc.dram_tensor("bias", [1, D], F32, kind="ExternalInput").ap()
    x1_out = nc.dram_tensor("x1", [TPC, D], F32, kind="ExternalOutput").ap()
    xn1_out = nc.dram_tensor("xn1", [TPC, D], BF16, kind="ExternalOutput").ap()
    xn1f_out = nc.dram_tensor("xn1f", [TPC, D], F32, kind="ExternalOutput").ap()
    emit_combine(p, x_rows, s_rows, y_all, rout_pc, bias_in, x1_out, xn1_out,
                 xn1f_out)
    p.close()
    return nc



TCH = [(0, 512), (512, 512), (1024, 512), (1536, 512), (2048, 128)]
TCHL = [(0, 512), (512, 512), (1024, 512), (1536, 512)]


def emit_mamba(p, xn1h, wts, prevmask, invmask, s1_out, rg):
    nc = p.nc
    (inprojT, conv_w, conv_b, xprojT, dtprojT, dtprojb, A_log, Dv, outprojT) = wts
    S = Sched(p)
    A = S.add

    inproj_bf = nc.dram_tensor("ma_ipb", [D, 2 * DI], BF16, kind="Internal").ap()
    outproj_bf = nc.dram_tensor("ma_opb", [DI, D], BF16, kind="Internal").ap()
    dtproj_bf = nc.dram_tensor("ma_dtb", [64, DI], BF16, kind="Internal").ap()
    xproj_bf = nc.dram_tensor("ma_xpb", [DI, 64], BF16, kind="Internal").ap()
    xif_d = nc.dram_tensor("ma_xif", [DI, TPC], F32, kind="Internal").ap()
    xib_d = nc.dram_tensor("ma_xib", [DI, TPC], BF16, kind="Internal").ap()
    zsil_d = nc.dram_tensor("ma_zsil", [DI, TPC], F32, kind="Internal").ap()
    scan_d = nc.dram_tensor("ma_scan", [DI, TPC], F32, kind="Internal").ap()
    cum_d = nc.dram_tensor("ma_cum", [DI, TPC], F32, kind="Internal").ap()
    ygb_d = nc.dram_tensor("ma_ygb", [DI, TPC], BF16, kind="Internal").ap()
    carry_d = nc.dram_tensor("ma_carry", [DI, 2], F32, kind="Internal").ap()
    ncr = len(rg[0])
    carry_all = nc.dram_tensor("ma_carry_all", [ncr * DI, 2], F32,
                               kind="Internal").ap()

    xnTd = [p.sb(f"ma_xnT{i}", [128, 2176], BF16) for i in range(8)]
    ipc = [p.sb(f"ma_ipc{i}", [128, 8 * 128], BF16) for i in range(2)]
    xz = p.sb("ma_xz", [128, 2176])
    xzb = p.sb("ma_xzb", [128, 2048], BF16)
    cwsb = p.sb("ma_cw", [128, 16 * 4])
    cbsb = p.sb("ma_cb", [128, 16])
    dbsb = p.sb("ma_db", [128, 16])
    Asb = p.sb("ma_A", [128, 16])
    Dsb = p.sb("ma_D", [128, 16])
    onec = p.sb("ma_onec", [128, 2])
    ones2k = p.sb("ma_ones2k", [128, 2048])
    xpc = p.sb("ma_xpc", [128, 16 * 64], BF16)
    dtl_b = p.sb("ma_dtl", [64, 2048], BF16)
    dtpc = [p.sb(f"ma_dtpc{i}", [64, 128], BF16) for i in range(2)]
    dl = p.sb("ma_dl", [128, 2048])
    sc1 = p.sb("ma_sc1", [128, 2048])
    sc2 = p.sb("ma_sc2", [128, 2048])
    sc3 = p.sb("ma_sc3", [128, 2048])
    zs = p.sb("ma_zs", [128, 2048])
    carry_sb = p.sb("ma_carrysb", [128, 32])
    call_sb = p.sb("ma_call", [128, 32 * ncr])
    Hsb = p.sb("ma_H", [128, 16])
    aeff = p.sb("ma_aeff", [128, 16])
    seff = p.sb("ma_seff", [128, 16])
    pmask = p.sb("ma_pm", [128, 8])
    imask = p.sb("ma_im", [128, 8])
    opfull = p.sb("ma_opf", [128, 16 * 1024], BF16)
    ygcol = [p.sb(f"ma_ygcol{i}", [128, 512], BF16) for i in range(2)]
    s1ev = p.sb("ma_s1ev", [128, 512])
    s1tok = [p.sb(f"ma_s1tok{i}", [128, 128]) for i in range(2)]
    identf = p.sb("ma_identf", [128, 128])
    onesi = p.sb("ma_onesi", [128, 128])
    ps_m = [p.ps(f"ma_psm{i}", [128, 512]) for i in range(2)]
    ps_d = [p.ps(f"ma_psd{i}", [64, 512]) for i in range(4)]
    ps_t = [p.ps(f"ma_pst{i}", [128, 128]) for i in range(2)]

    A("gpsimd", lambda e: e.dma_start(out=inproj_bf, in_=inprojT), w=[inproj_bf], dma=True)
    A("gpsimd", lambda e: e.dma_start(out=outproj_bf, in_=outprojT), w=[outproj_bf], dma=True)
    A("gpsimd", lambda e: e.dma_start(out=dtproj_bf, in_=dtprojT), w=[dtproj_bf], dma=True)
    A("gpsimd", lambda e: e.dma_start(out=xproj_bf, in_=xprojT), w=[xproj_bf], dma=True)
    A("vector", lambda e: e.memset(onesi[:], 1.0), w=[onesi])
    A("vector", lambda e: e.memset(onec[:], 1.0), w=[onec])
    A("vector", lambda e: e.memset(ones2k[:], 1.0), w=[ones2k])
    A("vector", lambda e: e.memset(Hsb[:], 0.0), w=[(Hsb, "i")])
    A("gpsimd", lambda e: e.affine_select(identf[:], onesi[:], [[-1, 128]],
                                          ALU.is_equal, 0.0, base=0,
                                          channel_multiplier=1),
      r=[onesi], w=[identf])
    A("sync", lambda e: e.dma_start(
        out=cwsb[:].rearrange("p (c k) -> p c k", c=16),
        in_=conv_w.rearrange("(c p) k -> p c k", p=128)), w=[cwsb], dma=True)
    for (t_, src_) in [(cbsb, conv_b), (dbsb, dtprojb), (Asb, A_log), (Dsb, Dv)]:
        A("sync", lambda e, t_=t_, src_=src_: e.dma_start(
            out=t_[:].rearrange("p (c o) -> p c o", c=16),
            in_=src_.rearrange("(c p) o -> p c o", p=128),
            allow_slow_non_contiguous=True), w=[t_], dma=True)
    A("sync", lambda e: e.dma_start(out=pmask[:], in_=prevmask), w=[pmask], dma=True)
    A("sync", lambda e: e.dma_start(out=imask[:], in_=invmask), w=[imask], dma=True)
    A("sync", lambda e: e.dma_start(
        out=xpc[:].rearrange("p (c f) -> p c f", c=16),
        in_=xproj_bf.rearrange("(c p) f -> p c f", p=128)), r=[xproj_bf],
      w=[xpc], dma=True)
    A("scalar", lambda e: e.activation(Asb[:], Asb[:], AF.Exp), r=[Asb], w=[Asb])
    A("vector", lambda e: e.tensor_scalar(Asb[:], Asb[:], -1.0, None, ALU.mult),
      r=[Asb], w=[Asb])
    A("vector", lambda e: e.tensor_scalar(Asb[:], Asb[:], -1e-6, -10.0,
                                          ALU.min, ALU.max), r=[Asb], w=[Asb])
    for dc in range(8):
        A("sync", lambda e, dc=dc: e.dma_start(
            out=xnTd[dc][:], in_=xn1h[:, 128 * dc:128 * (dc + 1)], transpose=True),
          w=[xnTd[dc]], dma=True)

    for ec in range(32):
        pr = ec % 2
        A("sync", lambda e, ec=ec, pr=pr: e.dma_start(
            out=ipc[pr][:].rearrange("p (c f) -> p c f", c=8),
            in_=inproj_bf[:, 128 * ec:128 * (ec + 1)]
                .rearrange("(c p) f -> p c f", p=128)),
          r=[inproj_bf], w=[ipc[pr]], dma=True)
        for ti, (c0, w) in enumerate(TCH):
            def xzmm(e, ec=ec, pr=pr, ti=ti, c0=c0, w=w):
                for dc in range(8):
                    ii = e.matmul(ps_m[(ec * 5 + ti) % 2][:, 0:w],
                                  ipc[pr][:, 128 * dc:128 * (dc + 1)],
                                  xnTd[dc][:, c0:c0 + w],
                                  start=(dc == 0), stop=(dc == 7))
                return ii
            A("tensor", xzmm, r=[ipc[pr]] + [xnTd[dc] for dc in range(8)],
              w=[ps_m[(ec * 5 + ti) % 2]])
            A("vector", lambda e, ec=ec, ti=ti, c0=c0, w=w: e.tensor_copy(
                xz[:, c0:c0 + w], ps_m[(ec * 5 + ti) % 2][:, 0:w]),
              r=[ps_m[(ec * 5 + ti) % 2]], w=[(xz, f"{ti}")])
        xzr = [(xz, f"{ti}") for ti in range(5)]
        if ec < 16:
            A("vector", lambda e, ec=ec: e.tensor_scalar(
                sc1[:], xz[:, 125:125 + 2048], cwsb[:, 4 * ec:4 * ec + 1], None,
                ALU.mult), r=xzr + [cwsb], w=[sc1])
            for k in range(1, 4):
                A("vector", lambda e, ec=ec, k=k: e.scalar_tensor_tensor(
                    sc1[:], xz[:, 125 + k:125 + k + 2048],
                    cwsb[:, 4 * ec + k:4 * ec + k + 1], sc1[:], ALU.mult, ALU.add),
                  r=xzr + [cwsb, sc1], w=[sc1])
            A("vector", lambda e, ec=ec: e.tensor_scalar(
                sc1[:], sc1[:], cbsb[:, ec:ec + 1], None, ALU.add),
              r=[sc1, cbsb], w=[sc1])
            A("scalar", lambda e: e.activation(sc1[:], sc1[:], AF.Silu),
              r=[sc1], w=[sc1])
            A("vector", lambda e: e.tensor_copy(xzb[:], sc1[:]), r=[sc1], w=[xzb])
            A("sync", lambda e, ec=ec: e.dma_start(
                out=xif_d[128 * ec:128 * (ec + 1), :], in_=sc1[:]),
              r=[sc1], w=[(xif_d, f"{ec}")], dma=True)
            A("gpsimd", lambda e, ec=ec: e.dma_start(
                out=xib_d[128 * ec:128 * (ec + 1), :], in_=xzb[:]),
              r=[xzb], w=[(xib_d, f"{ec}")], dma=True)
        else:
            zc = ec - 16
            A("scalar", lambda e: e.activation(sc2[:], xz[:, 128:128 + 2048],
                                               AF.Silu), r=xzr, w=[sc2])
            A("sync", lambda e, zc=zc: e.dma_start(
                out=zsil_d[128 * zc:128 * (zc + 1), :], in_=sc2[:]),
              r=[sc2], w=[(zsil_d, f"{zc}")], dma=True)

    for ec in range(16):
        A("gpsimd", lambda e, ec=ec: e.dma_start(
            out=xzb[:], in_=xib_d[128 * ec:128 * (ec + 1), :]),
          r=[(xib_d, f"{ec}")], w=[(xzb, "dt")], dma=True)
        for ti, (c0, w) in enumerate(TCHL):
            A("tensor", lambda e, ec=ec, ti=ti, c0=c0, w=w: e.matmul(
                ps_d[ti][:], xpc[:, 64 * ec:64 * (ec + 1)], xzb[:, c0:c0 + w],
                start=(ec == 0), stop=(ec == 15)),
              r=[xpc, (xzb, "dt")], w=[ps_d[ti]])
    for ti, (c0, w) in enumerate(TCHL):
        A("vector", lambda e, ti=ti, c0=c0, w=w: e.tensor_copy(
            dtl_b[:, c0:c0 + w], ps_d[ti][:]), r=[ps_d[ti]], w=[(dtl_b, f"{ti}")])

    for ec in range(16):
        pr = ec % 2
        A("sync", lambda e, ec=ec, pr=pr: e.dma_start(
            out=dtpc[pr][:], in_=dtproj_bf[:, 128 * ec:128 * (ec + 1)]),
          r=[dtproj_bf], w=[dtpc[pr]], dma=True)
        for ti, (c0, w) in enumerate(TCHL):
            A("tensor", lambda e, ec=ec, pr=pr, ti=ti, c0=c0, w=w: e.matmul(
                ps_m[ti % 2][:], dtpc[pr][:], dtl_b[:, c0:c0 + w],
                start=True, stop=True),
              r=[dtpc[pr]] + [(dtl_b, f"{t}") for t in range(4)], w=[ps_m[ti % 2]])
            A("vector", lambda e, ec=ec, ti=ti, c0=c0: e.tensor_scalar(
                dl[:, c0:c0 + 512], ps_m[ti % 2][:], dbsb[:, ec:ec + 1], None,
                ALU.add), r=[ps_m[ti % 2], dbsb], w=[(dl, f"{ti}")])
        dlr = [(dl, f"{t}") for t in range(4)]
        A("scalar", lambda e: e.activation(sc2[:], dl[:], AF.Abs), r=dlr, w=[sc2])
        A("scalar", lambda e: e.activation(sc2[:], sc2[:], AF.Exp, scale=-1.0),
          r=[sc2], w=[sc2])
        A("scalar", lambda e: e.activation(sc2[:], sc2[:], AF.Ln,
                                           bias=onec[:, 0:1]), r=[sc2, onec], w=[sc2])
        A("gpsimd", lambda e: e.tensor_scalar(sc3[:], dl[:], 0.0, None, ALU.max),
          r=dlr, w=[sc3])
        A("vector", lambda e: e.tensor_add(sc2[:], sc2[:], sc3[:]),
          r=[sc2, sc3], w=[sc2])
        A("vector", lambda e: e.tensor_scalar(sc2[:], sc2[:], 1e-6, 10.0,
                                              ALU.max, ALU.min), r=[sc2], w=[sc2])
        A("vector", lambda e, ec=ec: e.tensor_scalar(
            sc2[:], sc2[:], Asb[:, ec:ec + 1], None, ALU.mult),
          r=[sc2, Asb], w=[sc2])
        A("scalar", lambda e: e.activation(sc2[:], sc2[:], AF.Exp), r=[sc2], w=[sc2])
        A("vector", lambda e: e.tensor_scalar(sc2[:], sc2[:], 1e-6, 1.0,
                                              ALU.max, ALU.min), r=[sc2], w=[sc2])
        A("sync", lambda e, ec=ec: e.dma_start(
            out=sc3[:], in_=xif_d[128 * ec:128 * (ec + 1), :]),
          r=[(xif_d, f"{ec}")], w=[(sc3, "xi")], dma=True)
        A("vector", lambda e: e.tensor_tensor_scan(sc1[:], sc2[:], sc3[:], 0.0,
                                                   ALU.mult, ALU.add),
          r=[sc2, (sc3, "xi")], w=[sc1])
        A("vector", lambda e: e.tensor_tensor_scan(sc3[:], sc2[:], ones2k[:], 1.0,
                                                   ALU.mult, ALU.mult),
          r=[sc2, ones2k], w=[(sc3, "cum")])
        A("vector", lambda e, ec=ec: e.tensor_copy(carry_sb[:, 2 * ec:2 * ec + 1],
                                                   sc3[:, 2047:2048]),
          r=[(sc3, "cum")], w=[(carry_sb, f"a{ec}")])
        A("vector", lambda e, ec=ec: e.tensor_copy(
            carry_sb[:, 2 * ec + 1:2 * ec + 2], sc1[:, 2047:2048]),
          r=[sc1], w=[(carry_sb, f"s{ec}")])
        A("sync", lambda e, ec=ec: e.dma_start(
            out=scan_d[128 * ec:128 * (ec + 1), :], in_=sc1[:]),
          r=[sc1], w=[(scan_d, f"{ec}")], dma=True)
        A("gpsimd", lambda e, ec=ec: e.dma_start(
            out=cum_d[128 * ec:128 * (ec + 1), :], in_=sc3[:]),
          r=[(sc3, "cum")], w=[(cum_d, f"{ec}")], dma=True)

    A("sync", lambda e: e.dma_start(
        out=carry_d.rearrange("(ec p) k -> p ec k", p=128),
        in_=carry_sb[:].rearrange("p (ec k) -> p ec k", k=2)),
      r=[(carry_sb, f"a{ec}") for ec in range(16)] +
        [(carry_sb, f"s{ec}") for ec in range(16)],
      w=[(carry_d, "x")], dma=True)
    A("gpsimd", lambda e: e.collective_compute(
        "AllGather", ALU.bypass, rg, [carry_d.opt()], [carry_all.opt()]),
      r=[(carry_d, "x")], w=[(carry_all, "x")])
    A("sync", lambda e: e.dma_start(
        out=call_sb[:].rearrange("p (c ec k) -> p c ec k", c=ncr, k=2),
        in_=carry_all.rearrange("(c ec p) k -> p c ec k", p=128, ec=16)),
      r=[(carry_all, "x")], w=[call_sb], dma=True)
    for cp in range(ncr):
        asl = call_sb[:, 32 * cp:32 * cp + 32].rearrange("p (ec k) -> p ec k", k=2)
        A("vector", lambda e, cp=cp, asl=asl: e.tensor_scalar(
            aeff[:].rearrange("p (ec o) -> p ec o", o=1), asl[:, :, 0:1],
            pmask[:, cp:cp + 1], None, ALU.mult), r=[call_sb, pmask], w=[aeff])
        A("vector", lambda e, cp=cp: e.tensor_scalar(
            aeff[:], aeff[:], imask[:, cp:cp + 1], None, ALU.add),
          r=[aeff, imask], w=[aeff])
        A("vector", lambda e, cp=cp, asl=asl: e.tensor_scalar(
            seff[:].rearrange("p (ec o) -> p ec o", o=1), asl[:, :, 1:2],
            pmask[:, cp:cp + 1], None, ALU.mult), r=[call_sb, pmask], w=[seff])
        A("vector", lambda e: e.tensor_mul(Hsb[:], Hsb[:], aeff[:]),
          r=[(Hsb, "i"), aeff], w=[(Hsb, "i")])
        A("vector", lambda e: e.tensor_add(Hsb[:], Hsb[:], seff[:]),
          r=[(Hsb, "i"), seff], w=[(Hsb, "i")])

    for ec in range(16):
        A("sync", lambda e, ec=ec: e.dma_start(
            out=sc1[:], in_=scan_d[128 * ec:128 * (ec + 1), :]),
          r=[(scan_d, f"{ec}")], w=[sc1], dma=True)
        A("sync", lambda e, ec=ec: e.dma_start(
            out=sc3[:], in_=cum_d[128 * ec:128 * (ec + 1), :]),
          r=[(cum_d, f"{ec}")], w=[(sc3, "cum")], dma=True)
        A("sync", lambda e, ec=ec: e.dma_start(
            out=sc2[:], in_=xif_d[128 * ec:128 * (ec + 1), :]),
          r=[(xif_d, f"{ec}")], w=[sc2], dma=True)
        A("gpsimd", lambda e, ec=ec: e.dma_start(
            out=zs[:], in_=zsil_d[128 * ec:128 * (ec + 1), :]),
          r=[(zsil_d, f"{ec}")], w=[zs], dma=True)
        A("vector", lambda e, ec=ec: e.scalar_tensor_tensor(
            sc1[:], sc3[:], Hsb[:, ec:ec + 1], sc1[:], ALU.mult, ALU.add),
          r=[sc1, (sc3, "cum"), (Hsb, "i")], w=[sc1])
        A("vector", lambda e, ec=ec: e.scalar_tensor_tensor(
            sc1[:], sc2[:], Dsb[:, ec:ec + 1], sc1[:], ALU.mult, ALU.add),
          r=[sc1, sc2, Dsb], w=[sc1])
        A("vector", lambda e: e.tensor_scalar(sc1[:], sc1[:], -10000.0, 10000.0,
                                              ALU.max, ALU.min), r=[sc1], w=[sc1])
        A("vector", lambda e: e.tensor_mul(sc1[:], sc1[:], zs[:]),
          r=[sc1, zs], w=[sc1])
        A("vector", lambda e: e.tensor_copy(xzb[:], sc1[:]), r=[sc1], w=[(xzb, "yg")])
        A("gpsimd", lambda e, ec=ec: e.dma_start(
            out=ygb_d[128 * ec:128 * (ec + 1), :], in_=xzb[:]),
          r=[(xzb, "yg")], w=[(ygb_d, f"{ec}")], dma=True)

    for ec in range(16):
        A("sync", lambda e, ec=ec: e.dma_start(
            out=opfull[:, 1024 * ec:1024 * (ec + 1)],
            in_=outproj_bf[128 * ec:128 * (ec + 1), :]),
          r=[outproj_bf], w=[(opfull, f"{ec}")], dma=True)
    for oc in range(8):
        for ti, (c0, w) in enumerate(TCHL):
            kk = oc * 4 + ti
            for ec in range(16):
                A("gpsimd", lambda e, ec=ec, c0=c0, w=w, kk=kk: e.dma_start(
                    out=ygcol[(kk * 16 + ec) % 2][:],
                    in_=ygb_d[128 * ec:128 * (ec + 1), c0:c0 + w]),
                  r=[(ygb_d, f"{ec}")], w=[ygcol[(kk * 16 + ec) % 2]], dma=True)
                A("tensor", lambda e, oc=oc, ec=ec, kk=kk: e.matmul(
                    ps_m[kk % 2][:], opfull[:, 1024 * ec + 128 * oc:
                                            1024 * ec + 128 * (oc + 1)],
                    ygcol[(kk * 16 + ec) % 2][:], start=(ec == 0), stop=(ec == 15)),
                  r=[(opfull, f"{ec}"), ygcol[(kk * 16 + ec) % 2]],
                  w=[ps_m[kk % 2]])
            A("vector", lambda e, kk=kk: e.tensor_copy(s1ev[:], ps_m[kk % 2][:]),
              r=[ps_m[kk % 2]], w=[s1ev])
            for sub in range(4):
                A("tensor", lambda e, sub=sub, kk=kk: e.transpose(
                    ps_t[(kk * 4 + sub) % 2][:], s1ev[:, 128 * sub:128 * (sub + 1)],
                    identf[:]),
                  r=[s1ev, identf], w=[ps_t[(kk * 4 + sub) % 2]])
                A("vector", lambda e, sub=sub, kk=kk: e.tensor_copy(
                    s1tok[(kk * 4 + sub) % 2][:], ps_t[(kk * 4 + sub) % 2][:]),
                  r=[ps_t[(kk * 4 + sub) % 2]], w=[s1tok[(kk * 4 + sub) % 2]])
                A("sync", lambda e, oc=oc, c0=c0, sub=sub, kk=kk: e.dma_start(
                    out=s1_out[c0 + 128 * sub:c0 + 128 * (sub + 1),
                               128 * oc:128 * (oc + 1)],
                    in_=s1tok[(kk * 4 + sub) % 2][:]),
                  r=[s1tok[(kk * 4 + sub) % 2]], w=[(s1_out, f"{kk}.{sub}")],
                  dma=True)

    S.prepare()
    with nc.Block() as blk:
        S.emit(blk)
    return p


def build_mamba(rg=None):
    p = Prog()
    nc = p.nc
    xn1h = nc.dram_tensor("xn1h", [2176, D], BF16, kind="ExternalInput").ap()
    names = [("in_proj_wT", [D, 2 * DI]), ("conv_w", [DI, 4]), ("conv_b", [DI, 1]),
             ("x_proj_wT", [DI, 64]), ("dt_proj_wT", [64, DI]),
             ("dt_proj_b", [DI, 1]), ("A_log", [DI, 1]), ("Dv", [DI, 1]),
             ("out_proj_wT", [DI, D])]
    wts = [nc.dram_tensor(n, s, F32, kind="ExternalInput").ap() for n, s in names]
    prevmask = nc.dram_tensor("prevmask", [128, 8], F32, kind="ExternalInput").ap()
    invmask = nc.dram_tensor("invmask", [128, 8], F32, kind="ExternalInput").ap()
    s1_out = nc.dram_tensor("s1", [TPC, D], F32, kind="ExternalOutput").ap()
    emit_mamba(p, xn1h, tuple(wts), prevmask, invmask, s1_out,
               rg if rg is not None else [list(range(NCORES))])
    p.close()
    return nc



QKSC = 8.0


def emit_lffn(p, cur_h, posv, wts, newout, NTl=20, do_ffn=True):
    nc = p.nc
    (Wqkv, qs_in, ks_in, Wo, lng, lnb, W1p, b1p, W2p, b2) = wts
    S = Sched(p)
    A = S.add

    wqkv_bf = nc.dram_tensor("lf_wqkv", [D, 3072], BF16, kind="Internal").ap()
    wo_bf = nc.dram_tensor("lf_wo", [D, D], BF16, kind="Internal").ap()
    w1_bf = nc.dram_tensor("lf_w1", [D, 5632], BF16, kind="Internal").ap()
    w2_bf = nc.dram_tensor("lf_w2", [2816, D], BF16, kind="Internal").ap()
    curb_d = nc.dram_tensor("lf_curb", [2560, D], BF16, kind="Internal").ap()
    newc_d = nc.dram_tensor("lf_newc", [2560, D], F32, kind="Internal").ap()
    newcb_d = nc.dram_tensor("lf_newcb", [2560, D], BF16, kind="Internal").ap()
    ffnb_d = nc.dram_tensor("lf_ffnb", [2048, D], BF16, kind="Internal").ap()

    NT = NTl
    curt = [p.sb(f"lf_cur{i}", [128, D]) for i in range(2)]
    curbt = [p.sb(f"lf_curb{i}", [128, D], BF16) for i in range(2)]
    iof = p.sb("lf_iof", [1, 64])
    invf = p.sb("lf_invf", [1, 64], BF16)
    lnbs = p.sb("lf_lnbs", [1, 64], BF16)
    invf_b = p.sb("lf_invfb", [128, 64])
    lnbs_b = p.sb("lf_lnbsb", [128, 64])
    onesb1 = p.sb("lf_ones1", [1, 128], BF16)
    onesp = p.sb("lf_onesp", [128, 128])
    identf = p.sb("lf_identf", [128, 128])
    identb = p.sb("lf_identb", [128, 128], BF16)
    possb = p.sb("lf_pos", [128, 20])
    qsb = p.sb("lf_qs", [1, 64], BF16)
    ksb = p.sb("lf_ks", [1, 64], BF16)
    qs_b = p.sb("lf_qsb", [128, 64])
    ks_b = p.sb("lf_ksb", [128, 64])
    g_bc = p.sb("lf_gbc", [128, 1024])
    b_bc = p.sb("lf_bbc", [128, 1024])
    b2_bc = p.sb("lf_b2bc", [128, 1024])
    gvec = p.sb("lf_gvec", [1, 1024], BF16)
    bvec = p.sb("lf_bvec", [1, 1024], BF16)
    b2vec = p.sb("lf_b2vec", [1, 1024], BF16)
    b1sb = p.sb("lf_b1sb", [128, 44])
    lhs = [p.sb(f"lf_lhs{i}", [128, 8 * 128], BF16) for i in range(2)]
    wq_fc = [p.sb(f"lf_wqf{i}", [128, 8 * 512], BF16) for i in range(2)]
    qf = p.sb("lf_qf", [128, 1024])
    kf = p.sb("lf_kf", [128, 1024])
    m1s = p.sb("lf_m1", [128, 1024])
    m2s = p.sb("lf_m2", [128, 1024])
    sqh = p.sb("lf_sqh", [128, 1024])
    ssq = p.sb("lf_ssq", [128, 16])
    rstd = p.sb("lf_rstd", [128, 16])
    tb_th = p.sb("lf_th", [128, 64])
    tb_ps = p.sb("lf_ps2", [128, 64])
    tb_cos = p.sb("lf_cos", [128, 64])
    tb_sin = p.sb("lf_sin", [128, 64])
    tb_sc = p.sb("lf_sc", [128, 64])
    tb_T1 = p.sb("lf_T1", [128, 64])
    tb_T2 = p.sb("lf_T2", [128, 64])
    qnb = p.sb("lf_qnb", [128, 1024], BF16)
    knb = p.sb("lf_knb", [128, 1024], BF16)
    qT = p.sb("lf_qT", [64, 16 * 128], BF16)
    kTr = [p.sb(f"lf_kT{i}", [64, 16 * 128], BF16) for i in range(3)]
    v65r = [p.sb(f"lf_v65{i}", [128, 16 * 65], BF16) for i in range(3)]
    msk4 = [p.sb(f"lf_msk4{i}", [128, 512]) for i in range(3)]
    mscr4 = p.sb("lf_mscr4", [128, 512])
    zer512 = mscr4
    attb4 = [p.sb(f"lf_att4{i}", [128, 512], BF16) for i in range(3)]
    recz4 = p.sb("lf_recz4", [1, 512])
    kneg = p.sb("lf_kneg", [128, 20])
    reczb4 = p.sb("lf_reczb4", [1, 512], BF16)
    bcz4 = p.sb("lf_bcz4", [64, 512])
    aoT = p.sb("lf_aoT", [128, 8 * 128], BF16)
    shb = curt
    ffin = curbt
    fbT = p.sb("lf_fbT", [128, 8 * 512], BF16)
    w1_s = [p.sb(f"lf_w1s{i}", [128, 8 * 128], BF16) for i in range(2)]
    geb = p.sb("lf_geb", [128, 512], BF16)
    geglu = p.sb("lf_geglu", [128, 22 * 512], BF16)
    w2_s = [p.sb(f"lf_w2s{i}", [128, 1024], BF16) for i in range(2)]
    n2t = [p.sb(f"lf_n2t{i}", [128, D]) for i in range(2)]
    stt_ = p.sb("lf_stat", [128, 8])
    cpool = p.sb("lf_cpool", [128, 2])
    ps_q = [p.ps(f"lf_psq{i}", [128, 512]) for i in range(4)]
    ps_sc = [p.ps(f"lf_pssc{i}", [128, 512]) for i in range(2)]
    ps_c4 = p.ps("lf_psc4", [65, 512])
    ps_sb = p.ps("lf_pssb", [128, 128], BF16)

    A("gpsimd", lambda e: e.dma_start(out=wqkv_bf, in_=Wqkv), w=[wqkv_bf], dma=True)
    A("gpsimd", lambda e: e.dma_start(out=wo_bf, in_=Wo), w=[wo_bf], dma=True)
    A("gpsimd", lambda e: e.dma_start(out=w1_bf, in_=W1p), w=[w1_bf], dma=True)
    A("gpsimd", lambda e: e.dma_start(out=w2_bf, in_=W2p), w=[w2_bf], dma=True)
    A("vector", lambda e: e.memset(onesb1[:], 1.0), w=[onesb1])
    A("vector", lambda e: e.memset(onesp[:], 1.0), w=[onesp])
    A("vector", lambda e: e.memset(zer512[:], 0.0), w=[zer512])
    A("vector", lambda e: e.memset(cpool[:], 0.0), w=[cpool])
    A("vector", lambda e: e.memset(cpool[:, 0:1], LN_EPS), w=[cpool])
    for i in range(3):
        A("vector", lambda e, i=i: e.memset(v65r[i][:], 1.0), w=[(v65r[i], "init")])
    A("gpsimd", lambda e: e.affine_select(identf[:], onesp[:], [[-1, 128]],
                                          ALU.is_equal, 0.0, base=0,
                                          channel_multiplier=1),
      r=[onesp], w=[identf])
    A("vector", lambda e: e.tensor_copy(identb[:], identf[:]), r=[identf],
      w=[identb])
    A("sync", lambda e: e.dma_start(out=possb[:], in_=posv), w=[possb], dma=True)
    A("vector", lambda e: e.tensor_scalar(kneg[:], possb[:], 0.0, -1e9,
                                          ALU.is_lt, ALU.mult),
      r=[possb], w=[kneg])
    for dt in range(3):
        A("gpsimd", lambda e, dt=dt: e.affine_select(
            msk4[dt][:].rearrange("p (c q) -> p c q", c=4),
            zer512[:].rearrange("p (c q) -> p c q", c=4),
            [[0, 4], [1, 128]], ALU.is_ge, -1e9,
            base=128 * dt, channel_multiplier=-1),
          r=[zer512], w=[(msk4[dt], "a")])
        A("gpsimd", lambda e, dt=dt: e.affine_select(
            mscr4[:].rearrange("p (c q) -> p c q", c=4),
            zer512[:].rearrange("p (c q) -> p c q", c=4),
            [[0, 4], [-1, 128]], ALU.is_ge, -1e9,
            base=256 - 128 * dt, channel_multiplier=1),
          r=[zer512], w=[mscr4]) if dt == 2 else None
        if dt == 2:
            A("vector", lambda e, dt=dt: e.tensor_add(msk4[dt][:], msk4[dt][:],
                                                      mscr4[:]),
              r=[(msk4[dt], "a"), mscr4], w=[(msk4[dt], "a")])
    A("gpsimd", lambda e: e.iota(invf_b[:].rearrange("p (a b) -> p a b", a=2),
                                 [[0, 2], [1, 32]], base=0, channel_multiplier=0,
                                 allow_small_or_imprecise_dtypes=True),
      w=[(invf_b, "i")])
    A("vector", lambda e: e.tensor_scalar(invf_b[:], invf_b[:],
                                          -2.0 * float(np.log(10000.0)) / 64.0,
                                          None, ALU.mult),
      r=[(invf_b, "i")], w=[(invf_b, "s")])
    A("scalar", lambda e: e.activation(invf_b[:], invf_b[:], AF.Exp),
      r=[(invf_b, "s")], w=[invf_b])
    A("gpsimd", lambda e: e.iota(lnbs_b[:].rearrange("p (a b) -> p a b", a=2),
                                 [[0, 2], [1, 32]], base=0, channel_multiplier=0,
                                 allow_small_or_imprecise_dtypes=True),
      w=[(lnbs_b, "i")])
    A("vector", lambda e: e.tensor_scalar(lnbs_b[:], lnbs_b[:], 2.0 / 89.6,
                                          25.6 / 89.6, ALU.mult, ALU.add),
      r=[(lnbs_b, "i")], w=[(lnbs_b, "s")])
    A("scalar", lambda e: e.activation(lnbs_b[:], lnbs_b[:], AF.Ln),
      r=[(lnbs_b, "s")], w=[lnbs_b])
    for (vec, srcd, dst, scl) in [(qsb, qs_in, qs_b, QKSC), (ksb, ks_in, ks_b, 1.0)]:
        A("gpsimd", lambda e, vec=vec, srcd=srcd: e.dma_start(
            out=vec[:], in_=srcd.rearrange("p o -> o p"),
            allow_slow_non_contiguous=True), w=[(vec, "l")], dma=True)
        A("tensor", lambda e, vec=vec: e.matmul(ps_q[1][:, 0:64], onesb1[:],
                                                vec[:], start=True, stop=True),
          r=[(vec, "l"), onesb1], w=[ps_q[1]])
        A("vector", lambda e, dst=dst, scl=scl: e.tensor_scalar(
            dst[:], ps_q[1][:, 0:64], scl, None, ALU.mult),
          r=[ps_q[1]], w=[dst])
    for (vec, srcd, dst) in [(gvec, lng, g_bc), (bvec, lnb, b_bc),
                             (b2vec, b2, b2_bc)]:
        A("gpsimd", lambda e, vec=vec, srcd=srcd: e.dma_start(
            out=vec[:], in_=srcd.rearrange("p o -> o p"),
            allow_slow_non_contiguous=True), w=[(vec, "l")], dma=True)
        for h in range(2):
            A("tensor", lambda e, vec=vec, h=h: e.matmul(
                ps_q[2][:], onesb1[:], vec[:, 512 * h:512 * (h + 1)],
                start=True, stop=True), r=[(vec, "l"), onesb1], w=[ps_q[2]])
            A("vector", lambda e, dst=dst, h=h: e.tensor_copy(
                dst[:, 512 * h:512 * (h + 1)], ps_q[2][:]),
              r=[ps_q[2]], w=[(dst, f"{h}")])
    A("sync", lambda e: e.dma_start(
        out=b1sb[:].rearrange("p (c o) -> p c o", c=44),
        in_=b1p.rearrange("(c p) o -> p c o", p=128),
        allow_slow_non_contiguous=True), w=[b1sb], dma=True)

    for t in range(NT):
        pr = t % 2
        A("sync", lambda e, t=t, pr=pr: e.dma_start(
            out=curt[pr][:], in_=cur_h[128 * t:128 * (t + 1), :]),
          w=[curt[pr]], dma=True)
        A("vector", lambda e, pr=pr: e.tensor_copy(curbt[pr][:], curt[pr][:]),
          r=[curt[pr]], w=[curbt[pr]])
        A("gpsimd", lambda e, t=t, pr=pr: e.dma_start(
            out=curb_d[128 * t:128 * (t + 1), :], in_=curbt[pr][:]),
          r=[curbt[pr]], w=[(curb_d, f"{t}")], dma=True)

    curb_all = [(curb_d, f"{t}") for t in range(NT)]


    def build_tables(t, kind):
        sgn = 1.0 if kind == 0 else -1.0
        A("vector", lambda e, t=t: e.tensor_scalar(
            tb_th[:], invf_b[:], possb[:, t:t + 1], None, ALU.mult),
          r=[invf_b, possb], w=[(tb_th, "th")])
        A("vector", lambda e: e.tensor_scalar(
            tb_ps[:], tb_th[:], 1.0 / (2 * np.pi), 12582912.0, ALU.mult, ALU.add),
          r=[(tb_th, "th")], w=[(tb_ps, "a")])
        A("vector", lambda e: e.tensor_scalar(tb_ps[:], tb_ps[:], 12582912.0,
                                              None, ALU.subtract),
          r=[(tb_ps, "a")], w=[(tb_ps, "b")])
        A("vector", lambda e: e.scalar_tensor_tensor(
            tb_ps[:], tb_ps[:], -2 * np.pi, tb_th[:], ALU.mult, ALU.add),
          r=[(tb_ps, "b"), (tb_th, "th")], w=[(tb_ps, "psi")])
        A("vector", lambda e: e.tensor_scalar(tb_ps[:], tb_ps[:],
                                              float(np.pi), float(-np.pi),
                                              ALU.min, ALU.max),
          r=[(tb_ps, "psi")], w=[(tb_ps, "psic")])
        A("scalar", lambda e: e.activation(tb_sin[:], tb_ps[:], AF.Sin),
          r=[(tb_ps, "psic")], w=[tb_sin])
        A("vector", lambda e: e.tensor_scalar(tb_ps[:], tb_th[:], 1.0,
                                              np.pi / 2, ALU.mult, ALU.add),
          r=[(tb_th, "th")], w=[(tb_ps, "c0")])
        A("vector", lambda e: e.tensor_scalar(
            tb_cos[:], tb_ps[:], 1.0 / (2 * np.pi), 12582912.0, ALU.mult, ALU.add),
          r=[(tb_ps, "c0")], w=[(tb_cos, "a")])
        A("vector", lambda e: e.tensor_scalar(tb_cos[:], tb_cos[:], 12582912.0,
                                              None, ALU.subtract),
          r=[(tb_cos, "a")], w=[(tb_cos, "b")])
        A("vector", lambda e: e.scalar_tensor_tensor(
            tb_cos[:], tb_cos[:], -2 * np.pi, tb_ps[:], ALU.mult, ALU.add),
          r=[(tb_cos, "b"), (tb_ps, "c0")], w=[(tb_cos, "psi")])
        A("vector", lambda e: e.tensor_scalar(tb_cos[:], tb_cos[:],
                                              float(np.pi), float(-np.pi),
                                              ALU.min, ALU.max),
          r=[(tb_cos, "psi")], w=[(tb_cos, "psic")])
        A("scalar", lambda e: e.activation(tb_cos[:], tb_cos[:], AF.Sin),
          r=[(tb_cos, "psic")], w=[(tb_cos, "v")])
        A("vector", lambda e, t=t, sgn=sgn: e.tensor_scalar(
            tb_sc[:, 0:1], possb[:, t:t + 1], sgn / 128.0, -sgn * 32.0,
            ALU.mult, ALU.add), r=[possb], w=[(tb_sc, "p")])
        A("vector", lambda e: e.tensor_mul(tb_sc[:], lnbs_b[:],
                                           tb_sc[:, 0:1]) if False else
          e.tensor_scalar(tb_sc[:], lnbs_b[:], tb_sc[:, 0:1], None, ALU.mult),
          r=[lnbs_b, (tb_sc, "p")], w=[(tb_sc, "l")])
        A("scalar", lambda e: e.activation(tb_sc[:], tb_sc[:], AF.Exp),
          r=[(tb_sc, "l")], w=[(tb_sc, "v")])
        A("vector", lambda e: e.tensor_mul(tb_T1[:], tb_cos[:], tb_sc[:]),
          r=[(tb_cos, "v"), (tb_sc, "v")], w=[tb_T1])
        A("vector", lambda e: e.tensor_mul(tb_T2[:], tb_sin[:], tb_sc[:]),
          r=[tb_sin, (tb_sc, "v")], w=[tb_T2])

    def seg16(ap2d, w=64):
        return ap2d.rearrange("p (s e) -> p s e", e=w)

    def bc16(tile, w=64):
        ap = tile[:]
        return bass.AP(ap.tensor, ap.offset, [ap.ap[0], [0, 16], [1, w]])

    def bc16s(tile, off, w):
        ap = tile[:]
        return bass.AP(ap.tensor, ap.offset + off, [ap.ap[0], [0, 16], [1, w]])

    def rstdbc(w=64):
        ap = rstd[:]
        return bass.AP(ap.tensor, ap.offset, [ap.ap[0], [1, 16], [0, w]])

    def qknorm(t, src_f, dst_b, scb, kind):
        A("scalar", lambda e, src_f=src_f: e.activation(sqh[:], src_f[:],
                                                        AF.Square),
          r=[src_f], w=[sqh])
        A("vector", lambda e: e.reduce_sum(ssq[:], seg16(sqh[:]), axis=AX.X),
          r=[sqh], w=[ssq])
        A("scalar", lambda e: e.activation(ssq[:], ssq[:], AF.Sqrt),
          r=[ssq], w=[(ssq, "n")])
        A("vector", lambda e: e.tensor_scalar(ssq[:], ssq[:], 1e-12, None,
                                              ALU.max), r=[(ssq, "n")],
          w=[(ssq, "m")])
        A("vector", lambda e: e.reciprocal(rstd[:], ssq[:]), r=[(ssq, "m")],
          w=[rstd])
        build_tables(t, kind)
        A("vector", lambda e, src_f=src_f, scb=scb: e.tensor_tensor(
            seg16(sqh[:]), seg16(src_f[:]), bc16(scb), ALU.mult),
          r=[src_f, scb], w=[(sqh, "qsc")])
        A("vector", lambda e: e.tensor_tensor(
            seg16(m1s[:]), seg16(sqh[:]), bc16(tb_T1), ALU.mult),
          r=[(sqh, "qsc"), tb_T1], w=[m1s])
        A("gpsimd", lambda e: e.tensor_tensor(
            bass.AP(m2s[:].tensor, m2s[:].offset, [m2s[:].ap[0], [64, 16], [1, 32]]),
            bass.AP(sqh[:].tensor, sqh[:].offset + 32,
                    [sqh[:].ap[0], [64, 16], [1, 32]]),
            bc16s(tb_T2, 0, 32), ALU.mult), r=[(sqh, "qsc"), tb_T2],
          w=[(m2s, "a")])
        A("gpsimd", lambda e: e.tensor_scalar(
            bass.AP(m2s[:].tensor, m2s[:].offset, [m2s[:].ap[0], [64, 16], [1, 32]]),
            bass.AP(m2s[:].tensor, m2s[:].offset, [m2s[:].ap[0], [64, 16], [1, 32]]),
            -1.0, None, ALU.mult), r=[(m2s, "a")], w=[(m2s, "a2")])
        A("gpsimd", lambda e: e.tensor_tensor(
            bass.AP(m2s[:].tensor, m2s[:].offset + 32,
                    [m2s[:].ap[0], [64, 16], [1, 32]]),
            bass.AP(sqh[:].tensor, sqh[:].offset,
                    [sqh[:].ap[0], [64, 16], [1, 32]]),
            bc16s(tb_T2, 32, 32), ALU.mult), r=[(sqh, "qsc"), tb_T2],
          w=[(m2s, "b")])
        A("vector", lambda e: e.tensor_add(m1s[:], m1s[:], m2s[:]),
          r=[m1s, (m2s, "a2"), (m2s, "b")], w=[(m1s, "s")])
        A("vector", lambda e, dst_b=dst_b: e.tensor_tensor(
            seg16(dst_b[:]), seg16(m1s[:]), rstdbc(), ALU.mult),
          r=[(m1s, "s"), rstd], w=[dst_b])

    for t in range(1, NT):
        pr = t % 2
        r0 = 128 * t
        def lhsld(e, t=t, pr=pr, r0=r0):
            return e.dma_start(out=lhs[pr][:].rearrange("p (c q) -> p c q", c=8),
                               in_=curb_d[r0:r0 + 128, :]
                               .rearrange("q (c p) -> p c q", p=128),
                               transpose=False, allow_slow_non_contiguous=True)
        for dc in range(8):
            roff = r0 - 1 if dc >= 4 else r0
            A("sync", lambda e, dc=dc, roff=roff, pr=pr: e.dma_start(
                out=lhs[pr][:, 128 * dc:128 * (dc + 1)],
                in_=curb_d[roff:roff + 128, 128 * dc:128 * (dc + 1)],
                transpose=True),
              r=curb_all, w=[(lhs[pr], f"{dc}")], dma=True)
        A("sync", lambda e, pr=pr: e.dma_start(
            out=wq_s[pr][:].rearrange("p (a b) -> p a b", a=1)
            if False else wq_s[pr][:], in_=wqkv_bf[0:128, :]) if False else None,
          dma=True) if False else None
        for fc in range(6):
            bk = ps_q[fc % 2]
            A("scalar", lambda e, fc=fc: e.dma_start(
                out=wq_fc[fc % 2][:].rearrange("p (c q) -> p c q", c=8),
                in_=wqkv_bf[:, 512 * fc:512 * (fc + 1)]
                .rearrange("(c p) q -> p c q", p=128)),
              r=[wqkv_bf], w=[wq_fc[fc % 2]], dma=True)
            for dc in range(8):
                A("tensor", lambda e, dc=dc, fc=fc, pr=pr, bk=bk: e.matmul(
                    bk[:], lhs[pr][:, 128 * dc:128 * (dc + 1)],
                    wq_fc[fc % 2][:, 512 * dc:512 * (dc + 1)],
                    start=(dc == 0), stop=(dc == 7)),
                  r=[(lhs[pr], f"{dc}"), wq_fc[fc % 2]], w=[bk])
            if fc < 2:
                A("vector", lambda e, fc=fc, bk=bk: e.tensor_copy(
                    qf[:, 512 * fc:512 * (fc + 1)], bk[:]), r=[bk], w=[qf])
            elif fc < 4:
                A("vector", lambda e, fc=fc, bk=bk: e.tensor_copy(
                    kf[:, 512 * (fc - 2):512 * (fc - 1)], bk[:]), r=[bk], w=[kf])
            else:
                h = fc - 4
                A("scalar", lambda e, h=h, t=t, bk=bk: e.activation(
                    bass.AP(v65r[t % 3][:].tensor,
                            v65r[t % 3][:].offset + 65 * 8 * h,
                            [v65r[t % 3][:].ap[0], [65, 8], [1, 64]]),
                    bk[:].rearrange("p (s e) -> p s e", e=64), AF.Identity),
                  r=[bk, (v65r[t % 3], "init")], w=[(v65r[t % 3], "v")])
        qknorm(t, qf, qnb, qs_b, 0)
        qknorm(t, kf, knb, ks_b, 1)
        for hp in range(8):
            A("tensor", lambda e, hp=hp: e.transpose(
                ps_sb[:], qnb[:, 128 * hp:128 * (hp + 1)], identb[:]),
              r=[qnb, identb], w=[ps_sb])
            for hf in range(2):
                A("vector", lambda e, hp=hp, hf=hf: e.tensor_copy(
                    qT[:, 128 * (2 * hp + hf):128 * (2 * hp + hf + 1)],
                    ps_sb[64 * hf:64 * (hf + 1), :]),
                  r=[ps_sb], w=[(qT, f"{2 * hp + hf}")])
            A("tensor", lambda e, hp=hp: e.transpose(
                ps_sb[:], knb[:, 128 * hp:128 * (hp + 1)], identb[:]),
              r=[knb, identb], w=[ps_sb])
            for hf in range(2):
                A("vector", lambda e, hp=hp, hf=hf, t=t: e.tensor_copy(
                    kTr[t % 3][:, 128 * (2 * hp + hf):128 * (2 * hp + hf + 1)],
                    ps_sb[64 * hf:64 * (hf + 1), :]),
                  r=[ps_sb], w=[(kTr[t % 3], f"{2 * hp + hf}")])
        if t < 3:
            continue
        for g in range(4):
            for dt in range(2, -1, -1):
                kt = t - dt
                bk = ps_sc[dt % 2]
                for j in range(4):
                    hd = 4 * g + j
                    A("tensor", lambda e, hd=hd, kt=kt, j=j, bk=bk:
                      e.matmul(
                        bk[:, 128 * j:128 * (j + 1)],
                        kTr[kt % 3][:, 128 * hd:128 * (hd + 1)],
                        qT[:, 128 * hd:128 * (hd + 1)],
                        start=True, stop=True),
                      r=[(kTr[kt % 3], f"{hd}"), (qT, f"{hd}")],
                      w=[(bk, f"{j}")])
                A("vector", lambda e, dt=dt, kt=kt, bk=bk: e.scalar_tensor_tensor(
                    mscr4[:], bk[:], kneg[:, kt:kt + 1], msk4[dt][:],
                    ALU.add, ALU.add),
                  r=[(bk, "0"), (bk, "1"), (bk, "2"), (bk, "3"), kneg,
                     (msk4[dt], "a")], w=[mscr4])
                A("scalar", lambda e, dt=dt: e.activation(
                    attb4[dt][:], mscr4[:], AF.Exp),
                  r=[mscr4], w=[attb4[dt]])
            for j in range(4):
                hd = 4 * g + j
                for dt in range(2, -1, -1):
                    kt = t - dt
                    A("tensor", lambda e, hd=hd, kt=kt, dt=dt, j=j: e.matmul(
                        ps_c4[:, 128 * j:128 * (j + 1)],
                        bass.AP(v65r[kt % 3][:].tensor,
                                v65r[kt % 3][:].offset + 65 * hd,
                                [v65r[kt % 3][:].ap[0], [1, 65]]),
                        attb4[dt][:, 128 * j:128 * (j + 1)],
                        start=(dt == 2), stop=(dt == 0)),
                      r=[(v65r[kt % 3], "v"), (v65r[kt % 3], "init"),
                         attb4[dt]],
                      w=[(ps_c4, f"{j}")])
            A("vector", lambda e: e.tensor_scalar(recz4[:], ps_c4[64:65, :],
                                                  1e-30, None, ALU.max),
              r=[(ps_c4, "0"), (ps_c4, "1"), (ps_c4, "2"), (ps_c4, "3")],
              w=[(recz4, "z")])
            A("vector", lambda e: e.reciprocal(recz4[:], recz4[:]),
              r=[(recz4, "z")], w=[recz4])
            A("vector", lambda e: e.tensor_copy(reczb4[:], recz4[:]),
              r=[recz4], w=[reczb4])
            A("tensor", lambda e: e.matmul(ps_sc[0][0:64, 0:512],
                                           onesb1[:, 0:64], reczb4[:],
                                           start=True, stop=True),
              r=[onesb1, reczb4], w=[(ps_sc[0], "bc")])
            A("vector", lambda e: e.tensor_copy(bcz4[:], ps_sc[0][0:64, 0:512]),
              r=[(ps_sc[0], "bc")], w=[bcz4])
            for j in range(4):
                hd = 4 * g + j
                hp, hs = hd // 2, 64 * (hd % 2)
                A("vector", lambda e, hp=hp, hs=hs, j=j: e.tensor_tensor(
                    aoT[:, 128 * hp:128 * (hp + 1)][hs:hs + 64, :],
                    ps_c4[0:64, 128 * j:128 * (j + 1)],
                    bcz4[:, 128 * j:128 * (j + 1)], ALU.mult),
                  r=[(ps_c4, f"{j}"), bcz4], w=[(aoT, f"{hd}")])
        A("sync", lambda e, t=t, pr=pr, r0=r0: e.dma_start(
            out=curt[pr][:], in_=cur_h[r0:r0 + 128, :]),
          w=[curt[pr]], dma=True)
        for ic in range(8):
            A("scalar", lambda e, ic=ic: e.dma_start(
                out=w2_s[ic % 2][:], in_=wo_bf[128 * ic:128 * (ic + 1), :]),
              r=[wo_bf], w=[w2_s[ic % 2]], dma=True)
            for oc in range(2):
                A("tensor", lambda e, ic=ic, oc=oc: e.matmul(
                    ps_q[oc][:], aoT[:, 128 * ic:128 * (ic + 1)],
                    w2_s[ic % 2][:, 512 * oc:512 * (oc + 1)],
                    start=(ic == 0), stop=(ic == 7)),
                  r=[(aoT, f"{h}") for h in range(16)] + [w2_s[ic % 2]],
                  w=[ps_q[oc]])
        for oc in range(2):
            A("vector", lambda e, oc=oc, pr=pr: e.tensor_tensor(
                curt[pr][:, 512 * oc:512 * (oc + 1)],
                ps_q[oc][:], curt[pr][:, 512 * oc:512 * (oc + 1)], ALU.add),
              r=[ps_q[oc], curt[pr]], w=[(curt[pr], "n")])
        A("vector", lambda e, pr=pr: e.tensor_copy(curbt[pr][:], curt[pr][:]),
          r=[(curt[pr], "n")], w=[curbt[pr]])
        A("sync", lambda e, t=t, pr=pr, r0=r0: e.dma_start(
            out=newc_d[r0:r0 + 128, :], in_=curt[pr][:]),
          r=[(curt[pr], "n")], w=[(newc_d, f"{t}")], dma=True)
        A("gpsimd", lambda e, t=t, pr=pr, r0=r0: e.dma_start(
            out=newcb_d[r0:r0 + 128, :], in_=curbt[pr][:]),
          r=[curbt[pr]], w=[(newcb_d, f"{t}")], dma=True)

    newc_all = [(newc_d, f"{t}") for t in range(3, NT)]
    newcb_all = [(newcb_d, f"{t}") for t in range(3, NT)]

    for t in (range(4, NT) if do_ffn else []):
        pr = t % 2
        r0 = 128 * t
        A("sync", lambda e, r0=r0, pr=pr: e.dma_start(
            out=shb[pr][:, 0:512], in_=newc_d[r0:r0 + 128, 0:512]),
          r=newc_all, w=[(shb[pr], "a")], dma=True)
        A("sync", lambda e, r0=r0, pr=pr: e.dma_start(
            out=shb[pr][:, 512:1024], in_=newc_d[r0 - 1:r0 + 127, 512:1024]),
          r=newc_all, w=[(shb[pr], "b")], dma=True)
        A("scalar", lambda e, pr=pr: e.activation(sqh[:], shb[pr][:], AF.Square,
                                                  accum_out=stt_[:, 0:1]),
          r=[(shb[pr], "a"), (shb[pr], "b")], w=[sqh, (stt_, "ssq")])
        A("vector", lambda e, pr=pr: e.reduce_sum(stt_[:, 1:2], shb[pr][:],
                                                  axis=AX.X),
          r=[(shb[pr], "a"), (shb[pr], "b")], w=[(stt_, "mu")])
        A("vector", lambda e: e.tensor_scalar(stt_[:, 1:2], stt_[:, 1:2],
                                              1.0 / D, None, ALU.mult),
          r=[(stt_, "mu")], w=[(stt_, "mu")])
        A("vector", lambda e: e.tensor_tensor(stt_[:, 2:3], stt_[:, 1:2],
                                              stt_[:, 1:2], ALU.mult),
          r=[(stt_, "mu")], w=[(stt_, "m2")])
        A("vector", lambda e: e.tensor_scalar(stt_[:, 0:1], stt_[:, 0:1],
                                              1.0 / D, None, ALU.mult),
          r=[(stt_, "ssq")], w=[(stt_, "ssq")])
        A("vector", lambda e: e.tensor_sub(stt_[:, 2:3], stt_[:, 0:1],
                                           stt_[:, 2:3]),
          r=[(stt_, "ssq"), (stt_, "m2")], w=[(stt_, "var")])
        A("vector", lambda e: e.tensor_scalar(stt_[:, 2:3], stt_[:, 2:3],
                                              cpool[:, 0:1], None, ALU.add),
          r=[(stt_, "var"), cpool], w=[(stt_, "var")])
        A("scalar", lambda e: e.activation(stt_[:, 3:4], stt_[:, 2:3], AF.Sqrt),
          r=[(stt_, "var")], w=[(stt_, "std")])
        A("vector", lambda e: e.reciprocal(stt_[:, 4:5], stt_[:, 3:4]),
          r=[(stt_, "std")], w=[(stt_, "rstd")])
        A("vector", lambda e: e.tensor_scalar(stt_[:, 5:6], stt_[:, 1:2],
                                              stt_[:, 4:5], -1.0, ALU.mult,
                                              ALU.mult),
          r=[(stt_, "mu"), (stt_, "rstd")], w=[(stt_, "nmr")])
        A("scalar", lambda e, pr=pr: e.activation(
            sqh[:], shb[pr][:], AF.Identity, bias=stt_[:, 5:6],
            scale=stt_[:, 4:5]),
          r=[(shb[pr], "a"), (shb[pr], "b"), (stt_, "nmr"), (stt_, "rstd")],
          w=[(sqh, "ln")])
        A("vector", lambda e: e.tensor_tensor(sqh[:], sqh[:], g_bc[:], ALU.mult),
          r=[(sqh, "ln"), (g_bc, "0"), (g_bc, "1")], w=[(sqh, "g")])
        A("vector", lambda e, pr=pr: e.tensor_add(m1s[:], sqh[:], b_bc[:]),
          r=[(sqh, "g"), (b_bc, "0"), (b_bc, "1")], w=[(m1s, "xn")])
        A("vector", lambda e, pr=pr: e.tensor_copy(ffin[pr][:], m1s[:]),
          r=[(m1s, "xn")], w=[ffin[pr]])
        A("gpsimd", lambda e, r0=r0, pr=pr: e.dma_start(
            out=ffnb_d[r0 - 512:r0 - 512 + 128, :], in_=ffin[pr][:]),
          r=[ffin[pr]], w=[(ffnb_d, f"{t}")], dma=True)
    ffnb_all = [(ffnb_d, f"{t}") for t in range(4, NT)]

    for tch in (range(4) if do_ffn else []):
        c0 = 512 * tch
        for dc in range(8):
            A("sync", lambda e, dc=dc, c0=c0: e.dma_start(
                out=fbT[:, 512 * dc:512 * (dc + 1)],
                in_=ffnb_d[c0:c0 + 512, 128 * dc:128 * (dc + 1)], transpose=True),
              r=ffnb_all, w=[(fbT, f"{dc}")], dma=True)
        fbT_all = [(fbT, f"{dc}") for dc in range(8)]
        for j in range(22):
            fcg = 22 + j
            A("gpsimd", lambda e, fcg=fcg: e.dma_start(
                out=w1_s[0][:].rearrange("p (c q) -> p c q", c=8),
                in_=w1_bf[:, 128 * fcg:128 * (fcg + 1)]
                .rearrange("(c p) q -> p c q", p=128)),
              r=[w1_bf], w=[w1_s[0]], dma=True)
            for dc in range(8):
                A("tensor", lambda e, dc=dc: e.matmul(
                    ps_q[0][:], w1_s[0][:, 128 * dc:128 * (dc + 1)],
                    fbT[:, 512 * dc:512 * (dc + 1)],
                    start=(dc == 0), stop=(dc == 7)),
                  r=[w1_s[0]] + fbT_all, w=[ps_q[0]])
            A("scalar", lambda e, fcg=fcg: e.activation(
                geb[:], ps_q[0][:], AF.Gelu, bias=b1sb[:, fcg:fcg + 1]),
              r=[ps_q[0], b1sb], w=[geb])
            A("gpsimd", lambda e, j=j: e.dma_start(
                out=w1_s[1][:].rearrange("p (c q) -> p c q", c=8),
                in_=w1_bf[:, 128 * j:128 * (j + 1)]
                .rearrange("(c p) q -> p c q", p=128)),
              r=[w1_bf], w=[w1_s[1]], dma=True)
            for dc in range(8):
                A("tensor", lambda e, dc=dc: e.matmul(
                    ps_q[1][:], w1_s[1][:, 128 * dc:128 * (dc + 1)],
                    fbT[:, 512 * dc:512 * (dc + 1)],
                    start=(dc == 0), stop=(dc == 7)),
                  r=[w1_s[1]] + fbT_all, w=[ps_q[1]])
            A("vector", lambda e, j=j: e.scalar_tensor_tensor(
                geglu[:, 512 * j:512 * (j + 1)], ps_q[1][:],
                b1sb[:, j:j + 1], geb[:], ALU.add, ALU.mult),
              r=[ps_q[1], b1sb, geb], w=[(geglu, f"{j}")])
        geglu_all = [(geglu, f"{j}") for j in range(22)]
        for half in range(2):
            for q in range(2):
                ts4 = 2 * half + q
                tr0 = c0 + 128 * ts4
                A("sync", lambda e, tr0=tr0, q=q: e.dma_start(
                    out=n2t[q][:], in_=newc_d[tr0 + 512:tr0 + 512 + 128, :]),
                  r=newc_all, w=[n2t[q]], dma=True)
                A("vector", lambda e, q=q: e.tensor_add(
                    n2t[q][:], n2t[q][:], b2_bc[:]),
                  r=[n2t[q], (b2_bc, "0"), (b2_bc, "1")], w=[(n2t[q], "b")])
            for oc in range(2):
                for hc in range(22):
                    A("scalar", lambda e, hc=hc, oc=oc: e.dma_start(
                        out=w2_s[hc % 2][:, 0:512],
                        in_=w2_bf[128 * hc:128 * (hc + 1),
                                  512 * oc:512 * (oc + 1)]),
                      r=[w2_bf], w=[w2_s[hc % 2]], dma=True)
                    for q in range(2):
                        ts4 = 2 * half + q
                        A("tensor", lambda e, hc=hc, ts4=ts4, q=q: e.matmul(
                            ps_q[2 + q][:],
                            geglu[:, 512 * hc:512 * (hc + 1)]
                                 [:, 128 * ts4:128 * (ts4 + 1)],
                            w2_s[hc % 2][:, 0:512],
                            start=(hc == 0), stop=(hc == 21)),
                          r=geglu_all + [w2_s[hc % 2]], w=[ps_q[2 + q]])
                for q in range(2):
                    A("vector", lambda e, oc=oc, q=q: e.tensor_tensor(
                        n2t[q][:, 512 * oc:512 * (oc + 1)], ps_q[2 + q][:],
                        n2t[q][:, 512 * oc:512 * (oc + 1)], ALU.add),
                      r=[ps_q[2 + q], (n2t[q], "b")], w=[(n2t[q], f"f{oc}")])
            for q in range(2):
                tr0 = c0 + 128 * (2 * half + q)
                A("gpsimd", lambda e, tr0=tr0, q=q: e.dma_start(
                    out=newout[tr0:tr0 + 128, :], in_=n2t[q][:]),
                  r=[(n2t[q], "f0"), (n2t[q], "f1")], w=[(newout, f"{tr0}")],
                  dma=True)

    S.prepare()
    with nc.Block() as blk:
        S.emit(blk)
    return p


def build_lffn(NTl=20, do_ffn=True):
    p = Prog()
    nc = p.nc
    cur_h = nc.dram_tensor("cur_h", [2560, D], F32, kind="ExternalInput").ap()
    posv = nc.dram_tensor("posv", [128, 20], F32, kind="ExternalInput").ap()
    names = [("Wqkv", [D, 3072]), ("q_scale", [64, 1]), ("k_scale", [64, 1]),
             ("Wo", [D, D]), ("ln_g", [D, 1]), ("ln_b", [D, 1]),
             ("W1p", [D, 5632]), ("b1p", [5632, 1]), ("W2p", [2816, D]),
             ("b2", [D, 1])]
    wts = [nc.dram_tensor(n, s, F32, kind="ExternalInput").ap() for n, s in names]
    newout = nc.dram_tensor("newout", [TPC, D], F32, kind="ExternalOutput").ap()
    emit_lffn(p, cur_h, posv, tuple(wts), newout, NTl, do_ffn)
    p.close()
    return nc



def emit_final(p, x1_in, s1_in, g1_in, x2_out):
    nc = p.nc
    S = Sched(p)
    A = S.add
    xt = [p.sb(f"fn_x{i}", [128, D]) for i in range(2)]
    st = [p.sb(f"fn_s{i}", [128, D]) for i in range(2)]
    gt = [p.sb(f"fn_g{i}", [128, D]) for i in range(2)]
    ot = [p.sb(f"fn_o{i}", [128, D]) for i in range(2)]
    for jj in range(TPC // 128):
        pr = jj % 2
        r0 = 128 * jj
        A("sync", lambda e, r0=r0, pr=pr: e.dma_start(
            out=xt[pr][:], in_=x1_in[r0:r0 + 128, :]), w=[xt[pr]], dma=True)
        A("sync", lambda e, r0=r0, pr=pr: e.dma_start(
            out=st[pr][:], in_=s1_in[r0:r0 + 128, :]), w=[st[pr]], dma=True)
        A("sync", lambda e, r0=r0, pr=pr: e.dma_start(
            out=gt[pr][:], in_=g1_in[r0:r0 + 128, :]), w=[gt[pr]], dma=True)
        A("scalar", lambda e, pr=pr: e.activation(gt[pr][:], gt[pr][:], AF.Silu),
          r=[gt[pr]], w=[(gt[pr], "s")])
        A("vector", lambda e, pr=pr: e.tensor_tensor(ot[pr][:], st[pr][:],
                                                     gt[pr][:], ALU.mult),
          r=[st[pr], (gt[pr], "s")], w=[(ot[pr], "m")])
        A("vector", lambda e, pr=pr: e.tensor_tensor(ot[pr][:], xt[pr][:],
                                                     ot[pr][:], ALU.add),
          r=[xt[pr], (ot[pr], "m")], w=[ot[pr]])
        A("sync", lambda e, r0=r0, pr=pr, jj=jj: e.dma_start(
            out=x2_out[r0:r0 + 128, :], in_=ot[pr][:]),
          r=[ot[pr]], w=[(x2_out, f"{jj}")], dma=True)
    S.prepare()
    with nc.Block() as blk:
        S.emit(blk)
    return p


def build_final():
    p = Prog()
    nc = p.nc
    x1_in = nc.dram_tensor("x1", [TPC, D], F32, kind="ExternalInput").ap()
    s1_in = nc.dram_tensor("s1", [TPC, D], F32, kind="ExternalInput").ap()
    g1_in = nc.dram_tensor("g1", [TPC, D], F32, kind="ExternalInput").ap()
    x2_out = nc.dram_tensor("x2", [TPC, D], F32, kind="ExternalOutput").ap()
    emit_final(p, x1_in, s1_in, g1_in, x2_out)
    p.close()
    return nc



def kernel(**inputs):
    from concourse.bass_utils import run_bass_kernel_spmd
    cores = list(range(NCORES))
    f32 = np.float32

    x = np.ascontiguousarray(np.asarray(inputs["L0_attn_x"], f32)
                             .reshape(NTOK, D))
    wr = np.asarray(inputs["L0_ffn_w_router"], f32)

    res = run_bass_kernel_spmd(build_prep(), prep_host_inputs(x, wr),
                               core_ids=cores)
    xn = np.asarray(res.results[0]["xn_all"])
    routing = np.asarray(res.results[0]["routing"])
    inv = np.asarray(res.results[0]["inv"]).reshape(-1)

    Wmla = {k[8:]: np.asarray(inputs[k], f32) for k in inputs
            if k.startswith("L0_attn_") and k != "L0_attn_x"}
    res = run_bass_kernel_spmd(build_mla(), mla_host_inputs(xn, Wmla),
                               core_ids=cores)
    s_tok = np.empty((NTOK, D), xn.dtype)
    for c in range(NCORES):
        sT = np.asarray(res.results[c]["s_coreT"])
        for i in range(4):
            j = 4 * c + i
            b, h = divmod(j, 16)
            s_tok[8192 * b:8192 * (b + 1), 64 * h:64 * (h + 1)] = \
                sT[64 * i:64 * (i + 1), :].T

    w1 = np.asarray(inputs["L0_ffn_w1"], f32)
    w2 = np.asarray(inputs["L0_ffn_w2"], f32)
    maps = []
    for c in range(NCORES):
        inv_pc = np.ascontiguousarray(
            inv[CAP * c:CAP * (c + 1)].reshape(32, 128).T.astype(f32))
        maps.append({"xn_all": xn, "inv_pc": inv_pc,
                     "w1": np.ascontiguousarray(w1[c]),
                     "w2": np.ascontiguousarray(w2[c])})
    res = run_bass_kernel_spmd(build_moe(), maps, core_ids=cores)
    y_all = np.concatenate([np.asarray(res.results[c]["y"])
                            for c in range(NCORES)], axis=0)

    bias = np.asarray(inputs["L0_ffn_bias"], f32).reshape(1, D)
    maps = []
    for c in range(NCORES):
        rp = np.empty((128, 64), f32)
        for jj in range(16):
            col = 16 * c + jj
            rp[:, 4 * jj + 0] = routing[:, col]
            rp[:, 4 * jj + 1] = routing[:, 128 + col]
            rp[:, 4 * jj + 2] = routing[:, 256 + col]
            rp[:, 4 * jj + 3] = routing[:, 384 + col]
        maps.append({"x_rows": x[TPC * c:TPC * (c + 1)],
                     "s_rows": np.ascontiguousarray(s_tok[TPC * c:TPC * (c + 1)]),
                     "y_all": y_all, "rout_pc": rp, "bias": bias})
    res = run_bass_kernel_spmd(build_combine(), maps, core_ids=cores)
    x1 = [np.asarray(res.results[c]["x1"]) for c in range(NCORES)]
    xn1 = [np.asarray(res.results[c]["xn1"]) for c in range(NCORES)]
    xn1f = [np.asarray(res.results[c]["xn1f"]) for c in range(NCORES)]

    Wm = {
        "in_proj_wT": np.ascontiguousarray(
            np.asarray(inputs["L1_attn_in_proj_w"], f32).T),
        "conv_w": np.ascontiguousarray(
            np.asarray(inputs["L1_attn_conv_w"], f32)[:, 0, :]),
        "conv_b": np.asarray(inputs["L1_attn_conv_b"], f32).reshape(DI, 1),
        "x_proj_wT": np.ascontiguousarray(
            np.asarray(inputs["L1_attn_x_proj_w"], f32).T),
        "dt_proj_wT": np.ascontiguousarray(
            np.asarray(inputs["L1_attn_dt_proj_w"], f32).T),
        "dt_proj_b": np.asarray(inputs["L1_attn_dt_proj_b"], f32).reshape(DI, 1),
        "A_log": np.asarray(inputs["L1_attn_A_log"], f32).reshape(DI, 1),
        "Dv": np.asarray(inputs["L1_attn_D"], f32).reshape(DI, 1),
        "out_proj_wT": np.ascontiguousarray(
            np.asarray(inputs["L1_attn_out_proj_w"], f32).T),
    }
    maps = []
    for c in range(NCORES):
        xn1h = np.empty((2176, D), xn1[0].dtype)
        if c % 4 == 0:
            xn1h[:128] = 0
        else:
            xn1h[:128] = xn1[c - 1][-128:]
        xn1h[128:] = xn1[c]
        pm = np.zeros((128, NCORES), f32)
        for cp in range(NCORES):
            if cp // 4 == c // 4 and cp < c:
                pm[:, cp] = 1.0
        m = dict(Wm)
        m["xn1h"] = xn1h
        m["prevmask"] = pm
        m["invmask"] = np.ascontiguousarray(1.0 - pm)
        maps.append(m)
    res = run_bass_kernel_spmd(build_mamba(), maps, core_ids=cores)
    s1 = [np.asarray(res.results[c]["s1"]) for c in range(NCORES)]

    DIN = 2730
    Wq6 = np.asarray(inputs["L1_ffn_Wqkv"], f32)
    qs6 = np.asarray(inputs["L1_ffn_q_scale"], f32)
    ks6 = np.asarray(inputs["L1_ffn_k_scale"], f32)
    Wo6 = np.asarray(inputs["L1_ffn_Wo"], f32)
    lg6 = np.asarray(inputs["L1_ffn_ln_g"], f32)
    lb6 = np.asarray(inputs["L1_ffn_ln_b"], f32)
    W16 = np.asarray(inputs["L1_ffn_W1"], f32)
    b16 = np.asarray(inputs["L1_ffn_b1"], f32)
    W26 = np.asarray(inputs["L1_ffn_W2"], f32)
    b26 = np.asarray(inputs["L1_ffn_b2"], f32)
    nc_lffn = build_lffn()
    cur = xn1f
    for i in range(6):
        W1p = np.zeros((D, 5632), f32)
        W1p[:, 0:DIN] = W16[i][:, 0:DIN]
        W1p[:, 2816:2816 + DIN] = W16[i][:, DIN:]
        b1p = np.zeros((5632, 1), f32)
        b1p[0:DIN, 0] = b16[i][0:DIN]
        b1p[2816:2816 + DIN, 0] = b16[i][DIN:]
        W2p = np.zeros((2816, D), f32)
        W2p[0:DIN] = W26[i]
        lay = {"Wqkv": np.ascontiguousarray(Wq6[i]),
               "q_scale": qs6[i].reshape(64, 1), "k_scale": ks6[i].reshape(64, 1),
               "Wo": np.ascontiguousarray(Wo6[i]),
               "ln_g": lg6[i].reshape(D, 1), "ln_b": lb6[i].reshape(D, 1),
               "W1p": W1p, "b1p": b1p, "W2p": W2p, "b2": b26[i].reshape(D, 1)}
        maps = []
        for c in range(NCORES):
            ch = np.empty((2560, D), f32)
            if c % 4 == 0:
                ch[:512] = 0
            else:
                ch[:512] = cur[c - 1][-512:]
            ch[512:] = cur[c]
            posv = np.ascontiguousarray(
                (np.arange(2560, dtype=f32) - 512 + (c % 4) * TPC)
                .reshape(20, 128).T)
            m = dict(lay)
            m["cur_h"] = ch
            m["posv"] = posv
            maps.append(m)
        res = run_bass_kernel_spmd(nc_lffn, maps, core_ids=cores)
        cur = [np.asarray(res.results[c]["newout"]) for c in range(NCORES)]

    maps = [{"x1": x1[c], "s1": s1[c], "g1": cur[c]} for c in range(NCORES)]
    res = run_bass_kernel_spmd(build_final(), maps, core_ids=cores)
    x2 = np.concatenate([np.asarray(res.results[c]["x2"])
                         for c in range(NCORES)], axis=0)
    return np.ascontiguousarray(x2.reshape(2, 8192, D))
